# version 24
# baseline (speedup 1.0000x reference)
import numpy as np
from contextlib import ExitStack

import concourse.bass as bass
import concourse.mybir as mybir
from concourse.bass_utils import run_bass_kernel_spmd

F32 = mybir.dt.float32
BF16 = mybir.dt.bfloat16
I8 = mybir.dt.int8
AF = mybir.ActivationFunctionType
ALU = mybir.AluOpType
AX = mybir.AxisListType

B, N, E, D, II = 2, 8192, 1024, 256, 2816
TOK = 2048
NCH = 16
NIC = II // 128
EXP_BIAS = -30.0
EXP_SCALE = 0.0625
NEG = -1.0e30
EPS = 1e-5
QINV = 1.0 / 127.0

GSTS = [[j, 7 - j, 8 + j, 15 - j] for j in range(4)]
SLOT_KEY = [GSTS[s // 4][s % 4] for s in range(16)]

ENGS = ("tensor", "vector", "scalar", "sync", "gpsimd")


class Ctr:
    def __init__(self, stack, nc, name):
        self.h = stack.enter_context(nc.semaphore(name))
        self.n = 0


class Prog:
    def __init__(self, nc):
        self.nc = nc
        self.streams = {e: [] for e in ENGS}

    def on(self, eng, fn):
        self.streams[eng].append(fn)

    def t(self, fn): self.on("tensor", fn)
    def v(self, fn): self.on("vector", fn)
    def s(self, fn): self.on("scalar", fn)
    def sy(self, fn): self.on("sync", fn)
    def g(self, fn): self.on("gpsimd", fn)

    def wait(self, eng, ctr, val=None):
        tgt = ctr.n if val is None else val
        self.on(eng, lambda e, h=ctr.h, v=tgt: e.wait_ge(h, v))

    def flush(self):
        active = [e for e in ENGS if self.streams[e]]
        if not active:
            return
        with self.nc.Block() as block:
            for e in active:
                fns = self.streams[e]

                def runner(eng, fns=fns):
                    for f in fns:
                        f(eng)
                getattr(block, e)(runner)
        self.streams = {e: [] for e in ENGS}


def emit(P, eng, ctr, fn, k=1):
    if ctr is None:
        P.on(eng, fn)
        return None
    ctr.n += k

    def wrapped(e, fn=fn, h=ctr.h, k=k):
        fn(e).then_inc(h, k)
    P.on(eng, wrapped)
    return ctr.n


def build():
    nc = bass.Bass("TRN2", target_bir_lowering=False, detect_race_conditions=False)

    x_in = nc.dram_tensor("x_own", [TOK, E], F32, kind="ExternalInput").ap()
    wq_in = nc.dram_tensor("wq", [E, D], F32, kind="ExternalInput").ap()
    wk_in = nc.dram_tensor("wk", [E, D], F32, kind="ExternalInput").ap()
    wvd_in = nc.dram_tensor("wvd", [E, D], F32, kind="ExternalInput").ap()
    wvu_in = nc.dram_tensor("wvu", [D, E], F32, kind="ExternalInput").ap()
    cx_in = nc.dram_tensor("clip_x", [1, 1], F32, kind="ExternalInput").ap()
    cd_in = nc.dram_tensor("clip_down", [1, 1], F32, kind="ExternalInput").ap()
    wg_in = nc.dram_tensor("w_gate", [II, E], F32, kind="ExternalInput").ap()
    wu_in = nc.dram_tensor("w_up", [II, E], F32, kind="ExternalInput").ap()
    wd_in = nc.dram_tensor("w_down", [E, II], F32, kind="ExternalInput").ap()
    c_in = nc.dram_tensor("c_b", [E], F32, kind="ExternalInput").ap()
    wmod_in = nc.dram_tensor("wmod_sl", [E, 1536], F32, kind="ExternalInput").ap()
    y_out = nc.dram_tensor("y_own", [TOK, E], F32, kind="ExternalOutput").ap()

    mod_ag_i = nc.dram_tensor("mod_ag_i", [1, 1536], F32)
    mod_ag_o = nc.dram_tensor("mod_ag_o", [4, 1536], F32)
    agk_i = nc.dram_tensor("agk_i", [2, 128, TOK], BF16)
    agk_o = nc.dram_tensor("agk_o", [8, 128, TOK], BF16)
    agv_i = nc.dram_tensor("agv_i", [16, 128, D], BF16)
    agv_o = nc.dram_tensor("agv_o", [64, 128, D], BF16)
    arx_i = nc.dram_tensor("arx_i", [1, 8], F32)
    arx_o = nc.dram_tensor("arx_o", [1, 8], F32)
    arg_i = nc.dram_tensor("arg_i", [1, 8], F32)
    arg_o = nc.dram_tensor("arg_o", [1, 8], F32)
    x2_dram = nc.dram_tensor("x2_spill", [TOK, E], F32)

    GRP = [[0, 1, 2, 3], [4, 5, 6, 7]]
    ALLC = [[0, 1, 2, 3, 4, 5, 6, 7]]

    st0 = ExitStack()
    with st0:
        def sb(name, shape, dt, stk=st0):
            return stk.enter_context(nc.sbuf_tensor(name, shape, dt))

        def pst(name, shape, stk):
            return stk.enter_context(nc.psum_tensor(name, shape, F32))

        consts = sb("consts", [128, 2], F32)
        idf32 = sb("idf32", [128, 128], F32)
        idb = sb("idb", [128, 128], BF16)
        ones_b = sb("ones_b", [128, 1], BF16)
        ones_f1 = sb("ones_f1", [1, 128], F32)
        xT = sb("xT", [128, 8 * TOK], BF16)
        stats = sb("stats", [128, 96], F32)
        sums = sb("sums", [128, 32], F32)
        mod_pp = sb("mod_pp", [128, 48], F32)
        ca1_pp = sb("ca1_pp", [128, 8], F32)
        uvvec = sb("uvvec", [128, 12], F32)
        scal = sb("scal", [128, 4], F32)
        sc11 = sb("sc11", [1, 12], F32)
        smax = sb("smax", [1, 8], F32)
        hmax = sb("hmax", [128, 2], F32)
        rowbuf = sb("rowbuf", [1, 128], F32)
        gbrow = sb("gbrow", [1, 1024], F32)
        i8big = sb("i8big", [128, E], I8)

        sSY = Ctr(st0, nc, "sSY")
        sSC = Ctr(st0, nc, "sSC")
        sGP = Ctr(st0, nc, "sGP")
        sT = Ctr(st0, nc, "sT")
        sT2 = Ctr(st0, nc, "sT2")
        sV = Ctr(st0, nc, "sV")
        sV2 = Ctr(st0, nc, "sV2")
        sS = Ctr(st0, nc, "sS")
        sS2 = Ctr(st0, nc, "sS2")
        sCC = Ctr(st0, nc, "sCC")
        sGB = Ctr(st0, nc, "sGB")
        sWQ = Ctr(st0, nc, "sWQ")
        sY2 = [Ctr(st0, nc, f"sY2_{i}") for i in range(2)]
        sSP = Ctr(st0, nc, "sSP")
        sX2 = [Ctr(st0, nc, f"sX2_{i}") for i in range(2)]

        nc.const_aps.register(F32, EXP_BIAS, consts[:, 0:1])
        nc.const_aps.register(F32, EXP_SCALE, consts[:, 1:2])

        psum_tr = pst("psum_tr", [128, 128], st0)

        st_x = ExitStack()
        x_res = st_x.enter_context(nc.sbuf_tensor("x_res", [128, NCH * E], F32, side="right"))

        st_att = ExitStack()
        QT = st_att.enter_context(nc.sbuf_tensor("QT", [128, 2 * TOK], BF16))
        wvu_ga = st_att.enter_context(nc.sbuf_tensor("wvu_ga", [128, 2 * E], BF16))

        P = Prog(nc)

        def bcast_piece(colap, dst_slice, bank_slice):
            emit(P, "tensor", sT, lambda e, colap=colap: e.transpose(
                psum_tr[0:1, :], colap, idf32[:, :]))
            P.wait("vector", sT)
            emit(P, "vector", sV, lambda e: e.tensor_copy(rowbuf[:, :],
                                                          psum_tr[0:1, :]))
            P.wait("tensor", sV)
            emit(P, "tensor", sT, lambda e, bank_slice=bank_slice: e.matmul(
                bank_slice, ones_f1[:, :], rowbuf[:, :], start=True, stop=True))
            P.wait("vector", sT)
            emit(P, "vector", sV, lambda e, dst_slice=dst_slice, bank_slice=bank_slice:
                 e.tensor_copy(dst_slice, bank_slice))
            P.wait("vector", sV)
            P.wait("tensor", sV)

        p2 = ExitStack()
        scratch = sb("scratch", [128, E], F32, p2)
        wq_b = sb("wq_b", [128, 8 * D], BF16, p2)
        wk_b = sb("wk_b", [128, 8 * D], BF16, p2)
        wvd_b = sb("wvd_b", [128, 8 * D], BF16, p2)
        wvu_b = sb("wvu_b", [128, 2 * E], BF16, p2)
        cvec = sb("cvec", [128, 8], F32, p2)
        ctmp = sb("ctmp", [8, 128], F32, p2)
        modT = sb("modT", [48, 128], F32, p2)
        cvec_b = sb("cvec_b", [128, 8], BF16, p2)
        wmod_bf = [sb(f"wmodbf{i}", [128, 1536], BF16, p2) for i in range(2)]
        modsl = sb("modsl", [1, 1536], F32, p2)
        kst = sb("kst", [128, 2 * TOK], BF16, p2)
        vst = sb("vst", [128, 16 * D], BF16, p2)
        m_s = sb("m_s", [128, 512], F32, p2)
        r_s = sb("r_s", [128, 512], F32, p2)
        ga_bc = sb("ga_bc", [128, E], F32, p2)
        uvv_bc = sb("uvv_bc", [128, 2 * D], F32, p2)
        rowA = sb("rowA", [1, 1024], F32, p2)
        rowB = sb("rowB", [1, 1024], F32, p2)
        rowC = sb("rowC", [1, 1024], F32, p2)
        uvrow = sb("uvrow", [1, 512], F32, p2)
        ca1c = [sb(f"ca1c{k}", [128, 1], BF16, p2) for k in range(8)]
        sac = [sb(f"sac{k}", [128, 1], BF16, p2) for k in range(8)]
        ps_proj = pst("ps_proj", [128, 512], p2)
        ps_v = pst("ps_v", [128, D], p2)
        ps_bc = pst("ps_bc", [128, 512], p2)
        ps_uv = pst("ps_uv", [1, 512], p2)
        st_mod = ExitStack()
        ps_mod = [pst(f"ps_mod{i}", [1, 512], st_mod) for i in range(3)]

        x_loaded = emit(P, "sync", sSY, lambda e: e.dma_start(
            out=x_res[:, :],
            in_=bass.AP(x_in.tensor, 0, [[E, 128], [128 * E, NCH], [1, E]])), 16)
        P.wait("sync", sSY, x_loaded)
        emit(P, "gpsimd", sWQ, lambda e: e.dma_start(
            out=wq_b[:, :],
            in_=bass.AP(wq_in.tensor, 0, [[D, 128], [128 * D, 8], [1, D]])), 16)
        emit(P, "gpsimd", sWQ, lambda e: e.dma_start(
            out=wk_b[:, :],
            in_=bass.AP(wk_in.tensor, 0, [[D, 128], [128 * D, 8], [1, D]])), 16)
        emit(P, "gpsimd", sWQ, lambda e: e.dma_start(
            out=wvd_b[:, :],
            in_=bass.AP(wvd_in.tensor, 0, [[D, 128], [128 * D, 8], [1, D]])), 16)
        pw_loaded = emit(P, "gpsimd", sWQ, lambda e: e.dma_start(
            out=wvu_b[:, :],
            in_=bass.AP(wvu_in.tensor, 0, [[E, 128], [128 * E, 2], [1, E]])), 16)
        c_ld = emit(P, "scalar", sSC, lambda e: e.dma_start(
            out=ctmp[:, :], in_=bass.AP(c_in.tensor, 0, [[128, 8], [1, 128]])), 16)
        emit(P, "scalar", sSC, lambda e: e.dma_start(out=sc11[0:1, 0:1], in_=cx_in[:, :]), 16)
        clips_ld = emit(P, "scalar", sSC, lambda e: e.dma_start(
            out=sc11[0:1, 1:2], in_=cd_in[:, :]), 16)
        for fn in (lambda e: e.memset(consts[:, 0:1], EXP_BIAS),
                   lambda e: e.memset(consts[:, 1:2], EXP_SCALE),
                   lambda e: e.memset(ones_b[:, :], 1.0),
                   lambda e: e.memset(ones_f1[:, :], 1.0),
                   lambda e: e.memset(idf32[:, :], 1.0),
                   lambda e: e.memset(smax[:, :], 0.0),
                   lambda e: e.memset(hmax[:, :], 0.0)):
            P.v(fn)
        P.flush()

        id_done = emit(P, "gpsimd", sGB, lambda e: e.affine_select(
            out=idf32[:, :], in_=idf32[:, :], pattern=[[1, 128]],
            compare_op=ALU.is_equal, fill=0.0, base=0, channel_multiplier=-1), 1)
        P.wait("vector", sGB, id_done)
        emit(P, "vector", sV, lambda e: e.tensor_copy(idb[:, :], idf32[:, :]))

        P.wait("tensor", sSC, clips_ld)
        P.wait("tensor", sGB, id_done)
        emit(P, "tensor", sT, lambda e: e.transpose(
            psum_tr[:, 0:8], ctmp[:, :], idf32[0:8, 0:8]))
        P.wait("vector", sT)
        cv_done = emit(P, "vector", sV, lambda e: e.tensor_copy(cvec[:, :],
                                                                psum_tr[:, 0:8]))
        P.wait("scalar", sV, cv_done)
        silu_c = emit(P, "scalar", sS, lambda e: e.activation(
            out=cvec_b[:, :], in_=cvec[:, :], func=AF.Silu))

        P.wait("gpsimd", sWQ, pw_loaded)
        P.wait("tensor", sS, silu_c)
        wm_ld = {}
        for kc in range(8):
            if kc >= 2:
                P.wait("gpsimd", sT)
            wm_ld[kc] = emit(P, "gpsimd", sGP, lambda e, kc=kc: e.dma_start(
                out=wmod_bf[kc % 2][:, :],
                in_=bass.AP(wmod_in.tensor, kc * 128 * 1536, [[1536, 128], [1, 1536]])), 16)
            P.wait("gpsimd", sGP, wm_ld[kc])
            P.wait("tensor", sGP, wm_ld[kc])
            for n3 in range(3):
                fn = lambda e, kc=kc, n3=n3: e.matmul(
                    ps_mod[n3][:, :], cvec_b[:, kc:kc + 1],
                    wmod_bf[kc % 2][:, n3 * 512:(n3 + 1) * 512],
                    start=(kc == 0), stop=(kc == 7))
                if n3 == 2:
                    emit(P, "tensor", sT, fn)
                else:
                    P.t(fn)
        P.wait("scalar", sT)
        for n3 in range(3):
            fn = lambda e, n3=n3: e.copy(modsl[0:1, n3 * 512:(n3 + 1) * 512],
                                         ps_mod[n3][:, :])
            emit(P, "scalar", sS2, fn)
        P.wait("sync", sS2)
        agn = emit(P, "sync", sSY, lambda e: e.dma_start(
            out=mod_ag_i.ap(), in_=modsl[:, :]), 16)
        P.wait("gpsimd", sSY, agn)
        emit(P, "gpsimd", sCC, lambda e: e.collective_compute(
            "AllGather", ALU.bypass, replica_groups=GRP,
            ins=[mod_ag_i.ap().opt()], outs=[mod_ag_o.ap().opt()]))
        P.wait("gpsimd", sCC)
        modT_n = emit(P, "gpsimd", sGP, lambda e: e.dma_start(
            out=modT[:, :], in_=bass.AP(mod_ag_o, 0, [[128, 48], [1, 128]])), 16)
        P.wait("gpsimd", sGP, modT_n)
        ga_row_n = emit(P, "gpsimd", sGP, lambda e: e.dma_start(
            out=rowA[:, :], in_=bass.AP(mod_ag_o, 2 * 1024, [[1, 1], [1, 1024]])), 16)
        P.wait("gpsimd", sGP, ga_row_n)
        P.wait("tensor", sGP, modT_n)
        emit(P, "tensor", sT, lambda e: e.transpose(
            psum_tr[:, 0:48], modT[:, :], idf32[0:48, 0:48]))
        P.wait("vector", sT)
        mod_ready_v = emit(P, "vector", sV, lambda e: e.tensor_copy(
            mod_pp[:, :], psum_tr[:, 0:48]))

        P.wait("vector", sV, mod_ready_v)
        emit(P, "vector", sV, lambda e: e.tensor_scalar(
            out=ca1_pp[:, :], in0=mod_pp[:, 8:16],
            scalar1=1.0, scalar2=None, op0=ALU.add))
        P.wait("vector", sV)
        for k in range(8):
            P.v(lambda e, k=k: e.tensor_copy(ca1c[k][:, :], ca1_pp[:, k:k + 1]))
            fnc = lambda e, k=k: e.tensor_copy(sac[k][:, :], mod_pp[:, k:k + 1])
            if k == 7:
                uvr_ready = emit(P, "vector", sV, fnc)
            else:
                P.v(fnc)

        P.flush()

        P.wait("vector", sSY, x_loaded)
        P.wait("scalar", sSY, x_loaded)
        P.wait("gpsimd", sSY, x_loaded)
        for ch in range(NCH):
            fnr = lambda e, ch=ch: e.reduce_sum(
                out=sums[:, ch:ch + 1], in_=x_res[:, ch * E:(ch + 1) * E], axis=AX.X)
            if ch == NCH - 1:
                emit(P, "vector", sV, fnr)
            else:
                P.v(fnr)
            emit(P, "scalar", sS, lambda e, ch=ch: e.activation(
                out=scratch[:, :], in_=x_res[:, ch * E:(ch + 1) * E],
                func=AF.Square, accum_out=sums[:, 16 + ch:17 + ch]))
        sq_done = sS.n
        P.wait("vector", sV)
        emit(P, "vector", sV, lambda e: e.tensor_scalar(
            out=stats[:, 0:16], in0=sums[:, 0:16],
            scalar1=1.0 / E, scalar2=None, op0=ALU.mult))
        P.wait("vector", sV)
        P.wait("vector", sS, sq_done)
        emit(P, "vector", sV, lambda e: e.tensor_scalar(
            out=sums[:, 16:32], in0=sums[:, 16:32],
            scalar1=1.0 / E, scalar2=None, op0=ALU.mult))
        P.wait("vector", sV)
        emit(P, "vector", sV, lambda e: e.tensor_mul(stats[:, 80:96], stats[:, 0:16], stats[:, 0:16]))
        P.wait("vector", sV)
        emit(P, "vector", sV, lambda e: e.tensor_sub(stats[:, 16:32], sums[:, 16:32],
                                                     stats[:, 80:96]))
        P.wait("vector", sV)
        emit(P, "vector", sV, lambda e: e.tensor_scalar(
            out=stats[:, 16:32], in0=stats[:, 16:32], scalar1=EPS, scalar2=None, op0=ALU.add))
        P.wait("vector", sV)
        emit(P, "vector", sV, lambda e: e.reciprocal(out=stats[:, 80:96],
                                                     in_=stats[:, 16:32]))
        P.wait("vector", sV)
        P.wait("scalar", sV)
        r1_done = emit(P, "scalar", sS, lambda e: e.activation(
            out=stats[:, 16:32], in_=stats[:, 80:96], func=AF.Sqrt))

        ptr1 = [ps_proj, ps_bc]
        P.wait("tensor", sSY, x_loaded)
        evac_of1 = {}
        g1 = 0
        for s in range(4):
            for fc in range(8):
                pb = g1 % 2
                if g1 >= 2:
                    P.wait("tensor", sS2, evac_of1[g1 - 2])
                for cc in range(4):
                    fn = lambda e, s=s, fc=fc, cc=cc, pb=pb: e.transpose(
                        ptr1[pb][:, cc * 128:(cc + 1) * 128],
                        x_res[:, (s * 4 + cc) * E + fc * 128:
                              (s * 4 + cc) * E + fc * 128 + 128],
                        idf32[:, :])
                    if cc == 3:
                        emit(P, "tensor", sT2, fn)
                    else:
                        P.t(fn)
                P.wait("scalar", sT2)
                evac_of1[g1] = emit(P, "scalar", sS2, lambda e, s=s, fc=fc, pb=pb:
                                    e.activation(
                                        out=xT[:, fc * TOK + s * 512:
                                               fc * TOK + s * 512 + 512],
                                        in_=ptr1[pb][:, :], func=AF.Copy,
                                        bias=0.0, scale=1.0))
                g1 += 1
        xT_raw = sS2.n
        P.wait("tensor", sS2, xT_raw)

        P.wait("vector", sS2, xT_raw)
        for fc in range(8):
            emit(P, "vector", sV, lambda e, fc=fc: e.tensor_scalar(
                out=xT[:, fc * TOK:(fc + 1) * TOK], in0=xT[:, fc * TOK:(fc + 1) * TOK],
                scalar1=ca1_pp[:, fc:fc + 1], scalar2=None, op0=ALU.mult))
        xhat_done = sV.n

        P.wait("vector", sGP, ga_row_n)
        for fn in (lambda e: e.tensor_scalar(out=rowB[:, :], in0=rowA[:, :],
                                             scalar1=0.0, scalar2=None, op0=ALU.is_ge),
                   lambda e: e.tensor_scalar(out=rowB[:, :], in0=rowB[:, :],
                                             scalar1=2.0, scalar2=-1.0,
                                             op0=ALU.mult, op1=ALU.add),
                   lambda e: e.tensor_scalar(out=rowC[:, :], in0=rowA[:, :],
                                             scalar1=-1.0, scalar2=None, op0=ALU.mult),
                   lambda e: e.tensor_max(rowC[:, :], rowC[:, :], rowA[:, :]),
                   lambda e: e.tensor_scalar(out=rowC[:, :], in0=rowC[:, :],
                                             scalar1=0.25, scalar2=None, op0=ALU.max)):
            emit(P, "vector", sV, fn)
            P.wait("vector", sV)
        ga_done = emit(P, "vector", sV, lambda e: e.tensor_mul(
            rowA[:, :], rowC[:, :], rowB[:, :]))
        P.wait("tensor", sV, ga_done)
        for half in range(2):
            emit(P, "tensor", sT, lambda e, half=half: e.matmul(
                ps_bc[:, 0:512], ones_f1[:, :], rowA[0:1, half * 512:(half + 1) * 512],
                start=True, stop=True))
            P.wait("vector", sT)
            emit(P, "vector", sV, lambda e, half=half: e.tensor_copy(
                ga_bc[:, half * 512:(half + 1) * 512], ps_bc[:, 0:512]))
            P.wait("tensor", sV)
        P.wait("vector", sWQ, pw_loaded)
        for dv in range(2):
            P.v(lambda e, dv=dv: e.tensor_copy(scratch[:, :],
                                               wvu_b[:, dv * E:(dv + 1) * E]))
            emit(P, "vector", sV, lambda e, dv=dv: e.tensor_mul(
                wvu_ga[:, dv * E:(dv + 1) * E], scratch[:, :], ga_bc[:, :]))
        P.flush()

        st_mod.close()
        ps_proj2 = pst("ps_proj2", [128, 512], p2)
        ps_v2 = pst("ps_v2", [128, D], p2)
        ps_pj = [ps_proj, ps_proj2]
        ps_vv = [ps_v, ps_v2]
        P.wait("tensor", sV, uvr_ready)
        P.wait("tensor", sWQ, pw_loaded)
        for wi, wbuf in enumerate((wk_b, wq_b, wvd_b)):
            for half, cols in ((0, ca1c), (1, sac)):
                for kc in range(8):
                    fn = lambda e, wbuf=wbuf, half=half, cols=cols, kc=kc: e.matmul(
                        ps_uv[0:1, half * 256:(half + 1) * 256],
                        cols[kc][:, :],
                        wbuf[:, kc * D:(kc + 1) * D],
                        start=(kc == 0), stop=(kc == 7))
                    if kc == 7:
                        emit(P, "tensor", sT, fn)
                    else:
                        P.t(fn)
            P.wait("vector", sT)
            emit(P, "vector", sV, lambda e: e.tensor_copy(uvrow[:, :], ps_uv[0:1, :]))
            P.wait("tensor", sV)
            if wi < 2:
                for uv in range(2):
                    for dc in range(2):
                        emit(P, "tensor", sT, lambda e, uv=uv, dc=dc: e.transpose(
                            psum_tr[:, 0:1],
                            uvrow[0:1, uv * 256 + dc * 128: uv * 256 + dc * 128 + 128],
                            idf32[0:1, 0:1]))
                        P.wait("vector", sT)
                        emit(P, "vector", sV, lambda e, wi=wi, uv=uv, dc=dc:
                             e.tensor_copy(
                                 uvvec[:, 4 * wi + 2 * dc + uv:
                                       4 * wi + 2 * dc + uv + 1],
                                 psum_tr[:, 0:1]))
                        P.wait("vector", sV)
                        P.wait("tensor", sV)
            else:
                for uv in range(2):
                    emit(P, "tensor", sT, lambda e, uv=uv: e.matmul(
                        ps_bc[:, 0:256], ones_f1[:, :],
                        uvrow[0:1, uv * 256:(uv + 1) * 256], start=True, stop=True))
                    P.wait("vector", sT)
                    emit(P, "vector", sV, lambda e, uv=uv: e.tensor_copy(
                        uvv_bc[:, uv * 256:(uv + 1) * 256], ps_bc[:, 0:256]))
                    P.wait("tensor", sV)

        pj_state = {"n": 0}
        pj_ev = {}
        vd_ev = {}
        P.wait("vector", sS, r1_done)
        P.wait("tensor", sV, xhat_done)
        for mc in range(NCH):
            if mc >= 2:
                P.wait("tensor", sV, vd_ev[mc - 2])
            for kc in range(8):
                fn = lambda e, mc=mc, kc=kc: e.matmul(
                    ps_vv[mc % 2][:, :],
                    xT[:, kc * TOK + mc * 128: kc * TOK + (mc + 1) * 128],
                    wvd_b[:, kc * D:(kc + 1) * D],
                    start=(kc == 0), stop=(kc == 7))
                if kc == 7:
                    emit(P, "tensor", sT, fn)
                else:
                    P.t(fn)
            P.wait("vector", sT)
            vd_ev[mc] = emit(P, "vector", sV, lambda e, mc=mc: e.scalar_tensor_tensor(
                out=scratch[:, 0:D], in0=uvv_bc[:, 0:D],
                scalar=stats[:, mc:mc + 1], in1=ps_vv[mc % 2][:, :],
                op0=ALU.mult, op1=ALU.subtract))
            P.v(lambda e, mc=mc: e.tensor_scalar(
                out=scratch[:, 0:D], in0=scratch[:, 0:D],
                scalar1=stats[:, 16 + mc:17 + mc], scalar2=-1.0,
                op0=ALU.mult, op1=ALU.mult))
            emit(P, "vector", sV, lambda e, mc=mc: e.tensor_add(
                vst[:, mc * D:(mc + 1) * D], scratch[:, 0:D], uvv_bc[:, D:2 * D]))
        vd_done = sV.n
        P.wait("sync", sV, vd_done)
        bounce_v = emit(P, "sync", sSY, lambda e: e.dma_start(
            out=bass.AP(agv_i, 0, [[D, 128], [128 * D, 16], [1, D]]),
            in_=vst[:, :]), 16)
        P.wait("sync", sSY, bounce_v)
        P.wait("gpsimd", sSY, bounce_v)
        agv_done = emit(P, "gpsimd", sCC, lambda e: e.collective_compute(
            "AllGather", ALU.bypass, replica_groups=GRP,
            ins=[agv_i.ap().opt()], outs=[agv_o.ap().opt()]))

        P.wait("tensor", sS, r1_done)
        P.wait("tensor", sV, xhat_done)
        for nt in range(4):
            for half in range(2):
                dstt = (m_s, r_s)[half]
                for c4 in range(4):
                    c = nt * 4 + c4
                    bcast_piece(stats[:, 16 * half + c:16 * half + c + 1],
                                dstt[:, c4 * 128:(c4 + 1) * 128],
                                ps_bc[:, c4 * 128:(c4 + 1) * 128])
            for wi, wbuf in ((0, wk_b), (1, wq_b)):
                base_u = 0 if wi == 0 else 4
                for dc in range(2):
                    pj = pj_state["n"]
                    pjb = pj % 2
                    if pj >= 2:
                        P.wait("tensor", sV, pj_ev[pj - 2])
                    for kc in range(8):
                        fn = lambda e, wbuf=wbuf, dc=dc, nt=nt, kc=kc, pjb=pjb: \
                            e.matmul(
                            ps_pj[pjb][:, :],
                            wbuf[:, kc * D + dc * 128: kc * D + dc * 128 + 128],
                            xT[:, kc * TOK + nt * 512: kc * TOK + (nt + 1) * 512],
                            start=(kc == 0), stop=(kc == 7))
                        if kc == 7:
                            emit(P, "tensor", sT, fn)
                        else:
                            P.t(fn)
                    P.wait("vector", sT)
                    pj_ev[pj] = emit(P, "vector", sV,
                                     lambda e, base_u=base_u, dc=dc, pjb=pjb:
                                     e.scalar_tensor_tensor(
                        out=scratch[:, 0:512], in0=m_s[:, :],
                        scalar=uvvec[:, base_u + 2 * dc:base_u + 2 * dc + 1],
                        in1=ps_pj[pjb][:, :], op0=ALU.mult, op1=ALU.subtract))
                    P.v(lambda e: e.tensor_mul(scratch[:, 0:512], scratch[:, 0:512],
                                               r_s[:, :]))
                    dsl = (kst[:, dc * TOK + nt * 512: dc * TOK + (nt + 1) * 512]
                           if wi == 0 else
                           QT[:, dc * TOK + nt * 512: dc * TOK + (nt + 1) * 512])
                    emit(P, "vector", sV,
                         lambda e, dsl=dsl, base_u=base_u, dc=dc: e.tensor_scalar(
                             out=dsl, in0=scratch[:, 0:512], scalar1=-1.0,
                             scalar2=uvvec[:, base_u + 2 * dc + 1:base_u + 2 * dc + 2],
                             op0=ALU.mult, op1=ALU.add))
                    pj_state["n"] += 1
        proj_done = sV.n
        P.flush()

        for dc in range(2):
            emit(P, "sync", sSY, lambda e, dc=dc: e.dma_start(
                out=bass.AP(agk_i, dc * 128 * TOK, [[TOK, 128], [1, TOK]]),
                in_=kst[:, dc * TOK:(dc + 1) * TOK]), 16)
        bounce_n = sSY.n
        P.wait("sync", sSY, bounce_n)
        P.flush()
        p2.close()

        st_kv = ExitStack()
        KT = st_kv.enter_context(nc.sbuf_tensor("KT", [128, 2 * N], BF16))
        VdT = st_kv.enter_context(nc.sbuf_tensor("VdT", [128, 64 * D], BF16))
        masks = st_kv.enter_context(nc.sbuf_tensor("masks", [128, 4 * 512], F32))
        O_sb = st_kv.enter_context(nc.sbuf_tensor("O_sb", [128, 8 * 512], BF16))
        PT = [st_kv.enter_context(nc.sbuf_tensor(f"PT{i}", [128, 512], BF16))
              for i in range(2)]
        rl_tmp = st_kv.enter_context(nc.sbuf_tensor("rl_tmp", [1, 512], F32))

        P.v(lambda e: e.memset(masks[:, :], 0.0))
        P.flush()
        for ph in range(4):
            P.g(lambda e, ph=ph: e.affine_select(
                out=masks[:, ph * 512:(ph + 1) * 512],
                in_=masks[:, ph * 512:(ph + 1) * 512],
                pattern=[[1, 512]], compare_op=ALU.is_ge,
                fill=NEG, base=-128 * ph, channel_multiplier=-1))
        agk_done = emit(P, "gpsimd", sCC, lambda e: e.collective_compute(
            "AllGather", ALU.bypass, replica_groups=GRP,
            ins=[agk_i.ap().opt()], outs=[agk_o.ap().opt()]))
        P.wait("gpsimd", sCC, agv_done)
        kv_ready = emit(P, "gpsimd", sWQ, lambda e: e.dma_start(
            out=VdT[:, :],
            in_=bass.AP(agv_o, 0, [[D, 128], [128 * D, 64], [1, D]])), 16)
        P.wait("gpsimd", sCC, agk_done)
        for dc in range(2):
            emit(P, "gpsimd", sGP, lambda e, dc=dc: e.dma_start(
                out=KT[:, dc * N:(dc + 1) * N],
                in_=bass.AP(agk_o, dc * 128 * TOK,
                            [[TOK, 128], [2 * 128 * TOK, 4], [1, TOK]])), 16)
        kt_ready = sGP.n
        P.wait("gpsimd", sGP, kt_ready)
        P.flush()

        st_p3 = ExitStack()
        S2 = [pst(f"S2_{i}", [128, 512], st_p3) for i in range(2)]
        O2 = [pst(f"O2_{i}", [128, 512], st_p3) for i in range(2)]
        lps = pst("lps", [1, 512], st_p3)
        atp = [pst(f"atp{i}", [128, 512], st_p3) for i in range(2)]

        base = {"T": sT.n, "T2": sT2.n, "V": sV.n, "S": sS.n}

        def attn_body(j):
            t_l, v_l, s_l = [], [], []
            cnt = {"T": 0, "T2": 0, "V": 0, "S": 0}
            sinc_at = {}
            einc_at = {}
            linc_at = {}
            evac_after = {}
            sched = []
            for qs in range(4):
                gst = GSTS[j][qs]
                full = [sl for sl in range(16) if SLOT_KEY[sl] < gst]
                own = 4 * j + qs
                assert SLOT_KEY[own] == gst
                items = [(sl * 4 + cc, None) for sl in full for cc in range(4)]
                items += [(own * 4 + cc, cc) for cc in range(4)]
                for idx, (kc, mph) in enumerate(items):
                    sched.append((qs, kc, mph, idx == 0, idx == len(items) - 1))
            assert len(sched) == 136

            def emit_sblk(i):
                qs, kc, mph, first, last = sched[i]
                bank = i % 2
                if i >= 2:
                    t_l.append(("wait", sS.h, base["S"] + einc_at[i - 2]))
                for dc in range(2):
                    def mk(e, dc=dc, kc=kc, qs=qs, bank=bank):
                        return e.matmul(
                            S2[bank][:, :],
                            KT[:, dc * N + kc * 128: dc * N + (kc + 1) * 128],
                            QT[:, dc * TOK + qs * 512: dc * TOK + (qs + 1) * 512],
                            start=(dc == 0), stop=(dc == 1))
                    if dc == 1:
                        cnt["T"] += 1
                        t_l.append(("inc", sT.h, mk, 1))
                    else:
                        t_l.append(("do", mk))
                sinc_at[i] = cnt["T"]
                if mph is not None:
                    v_l.append(("wait", sT.h, base["T"] + sinc_at[i]))
                    def mkm(e, bank=bank, mph=mph):
                        return e.tensor_add(S2[bank][:, :], S2[bank][:, :],
                                            masks[:, mph * 512:(mph + 1) * 512])
                    cnt["V"] += 1
                    v_l.append(("inc", sV.h, mkm, 1))
                    s_l.append(("wait", sV.h, base["V"] + cnt["V"]))
                else:
                    s_l.append(("wait", sT.h, base["T"] + sinc_at[i]))
                if i >= 2:
                    s_l.append(("wait", sT2.h, base["T2"] + linc_at[i - 2]))
                def mke(e, bank=bank):
                    return e.activation(out=PT[bank][:, :], in_=S2[bank][:, :],
                                        func=AF.Exp, bias=EXP_BIAS, scale=EXP_SCALE)
                cnt["S"] += 1
                s_l.append(("inc", sS.h, mke, 1))
                einc_at[i] = cnt["S"]

            def emit_oblk(i):
                qs, kc, mph, first, last = sched[i]
                bank = i % 2
                if i == 0:
                    t_l.append(("wait", sWQ.h, kv_ready))
                t_l.append(("wait", sS.h, base["S"] + einc_at[i]))
                if first and qs > 0:
                    t_l.append(("wait", sV.h, base["V"] + evac_after[qs - 1]))
                for dvc in range(2):
                    def mko(e, dvc=dvc, kc=kc, bank=bank, first=first, last=last):
                        return e.matmul(
                            O2[dvc][:, :],
                            VdT[:, kc * D + dvc * 128: kc * D + dvc * 128 + 128],
                            PT[bank][:, :], start=first, stop=last)
                    t_l.append(("do", mko))
                def mkl(e, bank=bank, first=first, last=last):
                    return e.matmul(lps[:, :], ones_b[:, :], PT[bank][:, :],
                                    start=first, stop=last)
                cnt["T2"] += 1
                t_l.append(("inc", sT2.h, mkl, 1))
                linc_at[i] = cnt["T2"]
                if last:
                    v_l.append(("wait", sT2.h, base["T2"] + linc_at[i]))
                    for dvc in range(2):
                        def mkev(e, qs=qs, dvc=dvc):
                            return e.tensor_copy(
                                O_sb[:, qs * 1024 + dvc * 512:
                                     qs * 1024 + (dvc + 1) * 512], O2[dvc][:, :])
                        v_l.append(("do", mkev))
                    def mkl1(e, qs=qs):
                        return e.tensor_scalar(out=rl_tmp[0:1, :],
                                               in0=lps[:, :],
                                               scalar1=1e-30, scalar2=None, op0=ALU.max)
                    cnt["V"] += 1
                    v_l.append(("inc", sV.h, mkl1, 1))
                    v_l.append(("wait", sV.h, base["V"] + cnt["V"]))
                    def mkl2(e, qs=qs):
                        return e.reciprocal(out=rl_tmp[0:1, :], in_=rl_tmp[0:1, :])
                    cnt["V"] += 1
                    v_l.append(("inc", sV.h, mkl2, 1))
                    v_l.append(("wait", sV.h, base["V"] + cnt["V"]))
                    for cc in range(4):
                        t_l.append(("wait", sV.h, base["V"] + cnt["V"]))
                        def mtr(e, cc=cc):
                            return e.transpose(psum_tr[:, 0:1],
                                               rl_tmp[0:1, cc * 128:(cc + 1) * 128],
                                               idf32[0:1, 0:1])
                        cnt["T"] += 1
                        t_l.append(("inc", sT.h, mtr, 1))
                        v_l.append(("wait", sT.h, base["T"] + cnt["T"]))
                        def mtc(e, qs=qs, cc=cc):
                            return e.tensor_copy(stats[:, 64 + 4 * qs + cc:
                                                       65 + 4 * qs + cc],
                                                 psum_tr[:, 0:1])
                        cnt["V"] += 1
                        v_l.append(("inc", sV.h, mtc, 1))
                        t_l.append(("wait", sV.h, base["V"] + cnt["V"]))
                    evac_after[qs] = cnt["V"]

            for i in range(len(sched)):
                emit_sblk(i)
                if i >= 1:
                    emit_oblk(i - 1)
            emit_oblk(len(sched) - 1)
            return t_l, v_l, s_l, cnt["T"], cnt["T2"], cnt["V"], cnt["S"]

        bodies = [attn_body(j) for j in range(4)]
        assert len({b[3:] for b in bodies}) == 1, [b[3:] for b in bodies]
        nT_a, nT2_a, nV_a, nS_a = bodies[0][3:]

        def play(e, items):
            for it in items:
                if it[0] == "wait":
                    e.wait_ge(it[1], it[2])
                elif it[0] == "do":
                    it[1](e)
                else:
                    it[2](e).then_inc(it[1], it[3])

        kv_ready_n = kv_ready
        qt_ready = proj_done

        def mk_attn(which):
            def run(eng):
                if which == 0:
                    eng.wait_ge(sGP.h, kv_ready_n)
                    eng.wait_ge(sV.h, qt_ready)
                pid = eng.partition_id()
                idx = eng.compute_val(pid % 4)
                for j in eng.Switch(idx, 4):
                    play(eng, bodies[j][which])
            return run

        P.on("tensor", mk_attn(0))
        P.on("vector", mk_attn(1))
        P.on("scalar", mk_attn(2))
        sT.n += nT_a
        sT2.n += nT2_a
        sV.n += nV_a
        sS.n += nS_a
        P.flush()

        P.wait("tensor", sV)
        ep_ev = {}
        ke = 0
        for s in range(4):
            for mq in range(4):
                ch = s * 4 + mq
                for ec in range(2):
                    b2 = ke % 2
                    if ke >= 2:
                        P.wait("tensor", sV2, ep_ev[ke - 2])
                    for dvc in range(2):
                        fn = lambda e, s=s, mq=mq, ec=ec, dvc=dvc, b2=b2: e.matmul(
                            atp[b2][:, :],
                            O_sb[:, s * 1024 + dvc * 512 + mq * 128:
                                 s * 1024 + dvc * 512 + mq * 128 + 128],
                            wvu_ga[:, dvc * E + ec * 512: dvc * E + (ec + 1) * 512],
                            start=(dvc == 0), stop=(dvc == 1))
                        if dvc == 1:
                            emit(P, "tensor", sT, fn)
                        else:
                            P.t(fn)
                    P.wait("vector", sT)
                    ep_ev[ke] = emit(P, "vector", sV2,
                         lambda e, ch=ch, ec=ec, b2=b2: e.scalar_tensor_tensor(
                             out=x_res[:, ch * E + ec * 512: ch * E + ec * 512 + 512],
                             in0=atp[b2][:, :], scalar=stats[:, 64 + ch:65 + ch],
                             in1=x_res[:, ch * E + ec * 512: ch * E + ec * 512 + 512],
                             op0=ALU.mult, op1=ALU.add))
                    ke += 1
        attn_done = sV2.n
        P.flush()
        st_p3.close()
        st_kv.close()
        st_att.close()

        st_ffn = ExitStack()
        wdT = st_ffn.enter_context(nc.sbuf_tensor("wdT", [128, NIC * 1024], BF16))
        sduv = st_ffn.enter_context(nc.sbuf_tensor("sduv", [128, 12], F32))
        st_p45 = ExitStack()
        scr5 = st_p45.enter_context(nc.sbuf_tensor("scr5", [128, E], F32))
        cb1_bc = st_p45.enter_context(nc.sbuf_tensor("cb1_bc", [128, E], F32))
        sb_bc = st_p45.enter_context(nc.sbuf_tensor("sb_bc", [128, E], F32))
        rowA5 = st_p45.enter_context(nc.sbuf_tensor("rowA5", [1, 1024], F32))
        rowB5 = st_p45.enter_context(nc.sbuf_tensor("rowB5", [1, 1024], F32))
        rowC5 = st_p45.enter_context(nc.sbuf_tensor("rowC5", [1, 1024], F32))
        ps_bc2 = pst("ps_bc2", [128, 512], st_p45)
        pp2 = [pst(f"pp2_{i}", [128, 512], st_p45) for i in range(2)]
        pw = [st_p45.enter_context(nc.psum_tensor(f"pw{i}", [128, 512], BF16))
              for i in range(2)]
        st_wd = ExitStack()
        wdn = [st_wd.enter_context(nc.sbuf_tensor(f"wdn{i}", [128, II], F32))
               for i in range(2)]
        wdq = st_wd.enter_context(nc.sbuf_tensor("wdq", [128, II], BF16))
        i8w = st_wd.enter_context(nc.sbuf_tensor("i8w", [128, II], I8))

        spill_of = {}
        for q in range(4):
            spill_of[q] = emit(P, "sync", sSP, lambda e, q=q: e.dma_start(
                out=bass.AP(x2_dram, q * 4 * 128 * E, [[E, 128], [128 * E, 4], [1, E]]),
                in_=x_res[:, q * 4 * E:(q + 1) * 4 * E]), 16)
            P.wait("sync", sSP, spill_of[q])

        wd_ld = {}
        for hc0 in range(2):
            wd_ld[hc0] = emit(P, "scalar", sX2[hc0 % 2], lambda e, hc0=hc0: e.dma_start(
                out=wdn[hc0 % 2][:, :],
                in_=bass.AP(wd_in.tensor, hc0 * 128 * II, [[II, 128], [1, II]])), 16)

        for it_i, (vi, dstt) in enumerate(((4, cb1_bc), (3, sb_bc), (5, None))):
            if it_i > 0:
                P.wait("gpsimd", sT)
                P.wait("gpsimd", sV)
            rn = emit(P, "gpsimd", sGP, lambda e, vi=vi: e.dma_start(
                out=rowA5[:, :], in_=bass.AP(mod_ag_o, vi * 1024, [[1, 1], [1, 1024]])), 16)
            P.wait("gpsimd", sGP, rn)
            P.wait("vector", sGP, rn)
            if vi == 4:
                emit(P, "vector", sV, lambda e: e.tensor_scalar(
                    out=rowA5[:, :], in0=rowA5[:, :], scalar1=1.0, scalar2=None,
                    op0=ALU.add))
            elif vi == 5:
                for fn in (lambda e: e.tensor_scalar(out=rowB5[:, :], in0=rowA5[:, :],
                                                     scalar1=0.0, scalar2=None,
                                                     op0=ALU.is_ge),
                           lambda e: e.tensor_scalar(out=rowB5[:, :], in0=rowB5[:, :],
                                                     scalar1=2.0, scalar2=-1.0,
                                                     op0=ALU.mult, op1=ALU.add),
                           lambda e: e.tensor_scalar(out=rowC5[:, :], in0=rowA5[:, :],
                                                     scalar1=-1.0, scalar2=None,
                                                     op0=ALU.mult),
                           lambda e: e.tensor_max(rowC5[:, :], rowC5[:, :], rowA5[:, :]),
                           lambda e: e.tensor_scalar(out=rowC5[:, :], in0=rowC5[:, :],
                                                     scalar1=0.25, scalar2=None,
                                                     op0=ALU.max)):
                    emit(P, "vector", sV, fn)
                    P.wait("vector", sV)
                emit(P, "vector", sV, lambda e: e.tensor_mul(gbrow[:, :], rowC5[:, :],
                                                             rowB5[:, :]))
                continue
            else:
                emit(P, "vector", sV, lambda e: e.tensor_copy(rowB5[:, :], rowA5[:, :]))
            P.wait("tensor", sV)
            for half in range(2):
                emit(P, "tensor", sT, lambda e, half=half: e.matmul(
                    ps_bc2[:, :], ones_f1[:, :], rowA5[0:1, half * 512:(half + 1) * 512],
                    start=True, stop=True))
                P.wait("vector", sT)
                emit(P, "vector", sV, lambda e, dstt=dstt, half=half: e.tensor_copy(
                    dstt[:, half * 512:(half + 1) * 512], ps_bc2[:, :]))
                P.wait("tensor", sV)

        for ch in range(NCH):
            fnr = lambda e, ch=ch: e.reduce_sum(
                out=sums[:, ch:ch + 1], in_=x_res[:, ch * E:(ch + 1) * E], axis=AX.X)
            if ch == NCH - 1:
                emit(P, "vector", sV, fnr)
            else:
                P.v(fnr)
            emit(P, "scalar", sS, lambda e, ch=ch: e.activation(
                out=scr5[:, :], in_=x_res[:, ch * E:(ch + 1) * E],
                func=AF.Square, accum_out=sums[:, 16 + ch:17 + ch]))
        sq2_done = sS.n
        P.wait("vector", sV)
        emit(P, "vector", sV, lambda e: e.tensor_scalar(
            out=stats[:, 32:48], in0=sums[:, 0:16],
            scalar1=1.0 / E, scalar2=None, op0=ALU.mult))
        P.wait("vector", sV)
        P.wait("vector", sS, sq2_done)
        emit(P, "vector", sV, lambda e: e.tensor_scalar(
            out=sums[:, 16:32], in0=sums[:, 16:32],
            scalar1=1.0 / E, scalar2=None, op0=ALU.mult))
        P.wait("vector", sV)
        emit(P, "vector", sV, lambda e: e.tensor_mul(stats[:, 80:96], stats[:, 32:48], stats[:, 32:48]))
        P.wait("vector", sV)
        emit(P, "vector", sV, lambda e: e.tensor_sub(stats[:, 48:64], sums[:, 16:32],
                                                     stats[:, 80:96]))
        P.wait("vector", sV)
        emit(P, "vector", sV, lambda e: e.tensor_scalar(
            out=stats[:, 48:64], in0=stats[:, 48:64], scalar1=EPS, scalar2=None, op0=ALU.add))
        P.wait("vector", sV)
        emit(P, "vector", sV, lambda e: e.reciprocal(out=stats[:, 80:96],
                                                     in_=stats[:, 48:64]))
        P.wait("vector", sV)
        P.wait("scalar", sV)
        r2_done = emit(P, "scalar", sS, lambda e: e.activation(
            out=stats[:, 48:64], in_=stats[:, 80:96], func=AF.Sqrt))

        gw_state = {"gw": 0, "wdq_free": 0}
        wd_ev = {}

        def wd_hc(hc):
            gw = gw_state["gw"]
            P.wait("vector", sX2[hc % 2], wd_ld[hc])
            emit(P, "vector", sV, lambda e, hc=hc: e.tensor_reduce(
                out=sduv[:, hc:hc + 1], in_=wdn[hc % 2][:, :], axis=AX.X, op=ALU.max,
                apply_absolute_value=True))
            P.wait("vector", sV)
            emit(P, "vector", sV, lambda e, hc=hc: e.reciprocal(
                out=sums[:, 1:2], in_=sduv[:, hc:hc + 1]))
            P.wait("vector", sV)
            wq_sc = emit(P, "vector", sV, lambda e: e.tensor_scalar(
                out=sums[:, 1:2], in0=sums[:, 1:2], scalar1=127.0, scalar2=None,
                op0=ALU.mult))
            P.wait("vector", sV)
            P.wait("scalar", sV, wq_sc)
            cast_n = emit(P, "scalar", sS, lambda e, hc=hc: e.activation(
                out=i8w[:, :], in_=wdn[hc % 2][:, :], func=AF.Copy, bias=0.0,
                scale=sums[:, 1:2]))
            if hc + 2 < 8:
                P.wait("scalar", sS, cast_n)
                wd_ld[hc + 2] = emit(P, "scalar", sX2[hc % 2], lambda e, hc=hc:
                                     e.dma_start(
                    out=wdn[hc % 2][:, :],
                    in_=bass.AP(wd_in.tensor, (hc + 2) * 128 * II,
                                [[II, 128], [1, II]])), 16)
            P.wait("vector", sS, cast_n)
            if hc >= 1:
                P.wait("vector", sT2, gw_state["wdq_free"])
            wq_done = emit(P, "vector", sV, lambda e: e.tensor_copy(wdq[:, :],
                                                                    i8w[:, :]))
            P.wait("tensor", sV, wq_done)
            for grp in range(6):
                ics = list(range(grp * 4, min(grp * 4 + 4, NIC)))
                pbw = gw % 2
                if gw >= 2:
                    P.wait("tensor", sS2, wd_ev[gw - 2])
                for j, ic in enumerate(ics):
                    fn = lambda e, ic=ic, j=j, pbw=pbw: e.transpose(
                        pw[pbw][:, j * 128:(j + 1) * 128],
                        wdq[:, ic * 128:(ic + 1) * 128], idb[:, :])
                    if j == len(ics) - 1:
                        emit(P, "tensor", sT2, fn)
                    else:
                        P.t(fn)
                P.wait("scalar", sT2)
                for j, ic in enumerate(ics):
                    wd_ev[gw] = emit(P, "scalar", sS2, lambda e, ic=ic, j=j, hc=hc,
                                     pbw=pbw: e.activation(
                        out=wdT[:, ic * 1024 + hc * 128: ic * 1024 + hc * 128 + 128],
                        in_=pw[pbw][:, j * 128:(j + 1) * 128],
                        func=AF.Copy, bias=0.0, scale=1.0))
                gw += 1
            gw_state["gw"] = gw
            gw_state["wdq_free"] = sT2.n

        P.wait("vector", sS, r2_done)
        p1s = {}
        for ch in range(NCH):
            if ch % 2 == 1 and ch // 2 < 8:
                wd_hc(ch // 2)
            if ch % 4 == 0:
                P.wait("vector", sSP, spill_of[ch // 4])
            P.v(lambda e, ch=ch: e.tensor_scalar(
                out=x_res[:, ch * E:(ch + 1) * E], in0=x_res[:, ch * E:(ch + 1) * E],
                scalar1=stats[:, 32 + ch:33 + ch], scalar2=stats[:, 48 + ch:49 + ch],
                op0=ALU.subtract, op1=ALU.mult))
            P.v(lambda e, ch=ch: e.tensor_mul(x_res[:, ch * E:(ch + 1) * E],
                                              x_res[:, ch * E:(ch + 1) * E],
                                              cb1_bc[:, :]))
            P.v(lambda e, ch=ch: e.tensor_add(x_res[:, ch * E:(ch + 1) * E],
                                              x_res[:, ch * E:(ch + 1) * E],
                                              sb_bc[:, :]))
            emit(P, "vector", sV, lambda e, ch=ch: e.tensor_reduce(
                out=sums[:, 0:1], in_=x_res[:, ch * E:(ch + 1) * E], axis=AX.X,
                op=ALU.max, apply_absolute_value=True))
            P.wait("vector", sV)
            emit(P, "vector", sV, lambda e: e.tensor_max(hmax[:, 0:1], hmax[:, 0:1],
                                                         sums[:, 0:1]))
            if ch % 4 == 3:
                p1s[ch // 4] = sV.n
        emit(P, "vector", sV, lambda e: e.tensor_scalar(
            out=sduv[:, 0:8], in0=sduv[:, 0:8], scalar1=QINV, scalar2=None,
            op0=ALU.mult))
        P.wait("vector", sV)
        P.wait("tensor", sV)
        emit(P, "tensor", sT, lambda e: e.transpose(psum_tr[0:1, :], hmax[:, 0:1],
                                                    idf32[:, :]))
        P.wait("vector", sT)
        emit(P, "vector", sV, lambda e: e.tensor_copy(rowbuf[:, :], psum_tr[0:1, :]))
        P.wait("vector", sV)
        emit(P, "vector", sV, lambda e: e.reduce_max(out=smax[0:1, 0:1],
                                                     in_=rowbuf[:, :], axis=AX.X))
        P.wait("vector", sV)
        P.wait("sync", sV)
        arxn = emit(P, "sync", sSY, lambda e: e.dma_start(out=arx_i.ap(),
                                                          in_=smax[:, :]), 16)
        P.wait("gpsimd", sSY, arxn)
        emit(P, "gpsimd", sCC, lambda e: e.collective_compute(
            "AllReduce", ALU.max, replica_groups=ALLC,
            ins=[arx_i.ap().opt()], outs=[arx_o.ap().opt()]))
        P.wait("gpsimd", sCC)
        mx_ready = emit(P, "gpsimd", sGP, lambda e: e.dma_start(
            out=sc11[0:1, 2:3], in_=bass.AP(arx_o, 0, [[1, 1], [1, 1]])), 16)
        P.wait("gpsimd", sGP, mx_ready)
        P.wait("scalar", sGP, mx_ready)
        emit(P, "scalar", sS, lambda e: e.activation(out=sc11[0:1, 3:4],
                                                     in_=sc11[0:1, 0:1], func=AF.Sigmoid))
        P.wait("vector", sS)
        emit(P, "vector", sV, lambda e: e.tensor_mul(sc11[0:1, 4:5], sc11[0:1, 3:4],
                                                     sc11[0:1, 2:3]))
        P.wait("vector", sV)
        emit(P, "vector", sV, lambda e: e.tensor_scalar(
            out=sc11[0:1, 4:5], in0=sc11[0:1, 4:5],
            scalar1=QINV, scalar2=None, op0=ALU.mult))
        P.wait("vector", sV)
        emit(P, "vector", sV, lambda e: e.reciprocal(out=sc11[0:1, 5:6],
                                                     in_=sc11[0:1, 4:5]))
        P.wait("vector", sV)
        P.wait("tensor", sV)
        for srcc, dstc in ((5, 0), (4, 2)):
            emit(P, "tensor", sT, lambda e, srcc=srcc: e.matmul(
                ps_bc2[:, 0:1], ones_f1[:, :], sc11[0:1, srcc:srcc + 1],
                start=True, stop=True))
            P.wait("vector", sT)
            emit(P, "vector", sV, lambda e, dstc=dstc: e.tensor_copy(
                scal[:, dstc:dstc + 1], ps_bc2[:, 0:1]))
            P.wait("vector", sV)
            P.wait("tensor", sV)
        scal_ready = sV.n

        ev2 = {}
        vc2 = {}
        g2 = 0
        for s in range(4):
            P.wait("tensor", sV, p1s[s])
            for fc in range(8):
                pb2 = g2 % 2
                if g2 >= 2:
                    P.wait("tensor", sS2, ev2[g2 - 2])
                for cc in range(4):
                    fn = lambda e, s=s, fc=fc, cc=cc, pb2=pb2: e.transpose(
                        pp2[pb2][:, cc * 128:(cc + 1) * 128],
                        x_res[:, (s * 4 + cc) * E + fc * 128:
                              (s * 4 + cc) * E + fc * 128 + 128],
                        idf32[:, :])
                    if cc == 3:
                        emit(P, "tensor", sT2, fn)
                    else:
                        P.t(fn)
                P.wait("scalar", sT2)
                if g2 == 0:
                    P.wait("scalar", sV, scal_ready)
                if g2 >= 2:
                    P.wait("scalar", sV2, vc2[g2 - 2])
                ev2[g2] = emit(P, "scalar", sS2, lambda e, pb2=pb2: e.activation(
                    out=i8big[:, pb2 * 512:pb2 * 512 + 512], in_=pp2[pb2][:, :],
                    func=AF.Copy, bias=0.0, scale=scal[:, 0:1]))
                P.wait("vector", sS2, ev2[g2])
                vc2[g2] = emit(P, "vector", sV2, lambda e, s=s, fc=fc, pb2=pb2:
                               e.tensor_copy(
                                   xT[:, fc * TOK + s * 512: fc * TOK + s * 512 + 512],
                                   i8big[:, pb2 * 512:pb2 * 512 + 512]))
                g2 += 1
        xq_done = sV2.n
        P.wait("tensor", sV2, xq_done)
        spill_n = sSP.n
        P.flush()
        st_wd.close()
        st_p45.close()
        st_x.close()

        st_p5 = ExitStack()
        gT = st_p5.enter_context(nc.sbuf_tensor("gT", [128, NIC * TOK], BF16))
        vfin = st_p5.enter_context(nc.sbuf_tensor("vfin", [128, E], F32))
        vtmp = st_p5.enter_context(nc.sbuf_tensor("vtmp", [128, 128], F32))
        ps_bc3 = pst("ps_bc3", [128, 512], st_p5)
        st_pa = ExitStack()
        wgn = [st_pa.enter_context(nc.sbuf_tensor(f"wgn{i}", [128, E], F32))
               for i in range(2)]
        wun = [st_pa.enter_context(nc.sbuf_tensor(f"wun{i}", [128, E], F32))
               for i in range(2)]
        wgq = st_pa.enter_context(nc.sbuf_tensor("wgq", [128, E], BF16))
        wuq = st_pa.enter_context(nc.sbuf_tensor("wuq", [128, E], BF16))
        i8w2 = st_pa.enter_context(nc.sbuf_tensor("i8w2", [128, E], I8))
        wgT = [st_pa.enter_context(nc.sbuf_tensor(f"wgT{i}", [128, E], BF16))
               for i in range(2)]
        wuT = [st_pa.enter_context(nc.sbuf_tensor(f"wuT{i}", [128, E], BF16))
               for i in range(2)]
        gtmp = st_pa.enter_context(nc.sbuf_tensor("gtmp", [128, 512], F32))
        gps = [pst(f"gps{i}", [128, 512], st_pa) for i in range(2)]
        ups = [pst(f"ups{i}", [128, 512], st_pa) for i in range(2)]
        pwa = [st_pa.enter_context(nc.psum_tensor(f"pwa{i}", [128, 512], BF16))
               for i in range(2)]

        P.wait("scalar", sSP, spill_n)
        P.wait("vector", sSP, spill_n)
        P.wait("gpsimd", sSP, spill_n)
        P.wait("tensor", sSP, spill_n)

        sT_after_ic = {}
        wT_ready = {}
        wA_ld = {}
        wA_ev = {}
        wA_state = {"ga": 0}
        wq_tr_done = {}
        wA_cp = {}

        def ffnA_load(ic):
            buf = ic % 2
            emit(P, "scalar", sY2[buf], lambda e, ic=ic, buf=buf: e.dma_start(
                out=wgn[buf][:, :],
                in_=bass.AP(wg_in.tensor, ic * 128 * E, [[E, 128], [1, E]])), 16)
            wA_ld[ic] = emit(P, "scalar", sY2[buf], lambda e, ic=ic, buf=buf:
                             e.dma_start(
                out=wun[buf][:, :],
                in_=bass.AP(wu_in.tensor, ic * 128 * E, [[E, 128], [1, E]])), 16)

        def ffnA_prep(ic):
            buf = ic % 2
            P.wait("vector", sY2[buf], wA_ld[ic])
            P.v(lambda e, buf=buf: e.tensor_reduce(
                out=sums[:, 2:3], in_=wgn[buf][:, :], axis=AX.X, op=ALU.max,
                apply_absolute_value=True))
            emit(P, "vector", sV, lambda e, buf=buf: e.tensor_reduce(
                out=sums[:, 3:4], in_=wun[buf][:, :], axis=AX.X, op=ALU.max,
                apply_absolute_value=True))
            P.wait("vector", sV)
            P.v(lambda e: e.reciprocal(out=sums[:, 4:5], in_=sums[:, 2:3]))
            emit(P, "vector", sV, lambda e: e.reciprocal(out=sums[:, 5:6],
                                                         in_=sums[:, 3:4]))
            P.wait("vector", sV)
            emit(P, "vector", sV, lambda e: e.tensor_scalar(
                out=sums[:, 4:6], in0=sums[:, 4:6],
                scalar1=127.0, scalar2=None, op0=ALU.mult))
            P.wait("vector", sV)
            P.v(lambda e, buf=buf: e.tensor_scalar(
                out=sduv[:, 8 + 2 * buf:9 + 2 * buf], in0=sums[:, 2:3],
                scalar1=scal[:, 2:3], scalar2=QINV, op0=ALU.mult, op1=ALU.mult))
            wsc = emit(P, "vector", sV, lambda e, buf=buf: e.tensor_scalar(
                out=sduv[:, 9 + 2 * buf:10 + 2 * buf], in0=sums[:, 3:4],
                scalar1=scal[:, 2:3], scalar2=QINV, op0=ALU.mult, op1=ALU.mult))
            P.wait("scalar", sV, wsc)
            emit(P, "scalar", sS, lambda e, buf=buf: e.activation(
                out=i8big[:, :], in_=wgn[buf][:, :], func=AF.Copy, bias=0.0,
                scale=sums[:, 4:5]))
            P.wait("vector", sS)
            if ic >= 1:
                P.wait("vector", sT2, wq_tr_done[ic - 1])
            emit(P, "vector", sV, lambda e: e.tensor_copy(wgq[:, :], i8big[:, :]))
            cast2 = emit(P, "scalar", sS, lambda e, buf=buf: e.activation(
                out=i8w2[:, :], in_=wun[buf][:, :], func=AF.Copy, bias=0.0,
                scale=sums[:, 5:6]))
            P.wait("vector", sS)
            emit(P, "vector", sV, lambda e: e.tensor_copy(wuq[:, :], i8w2[:, :]))
            wA_cp[ic] = sV.n
            return cast2

        def ffnA_transpose(ic):
            buf = ic % 2
            P.wait("tensor", sV, wA_cp[ic])
            if ic >= 2:
                P.wait("tensor", sT, sT_after_ic[ic - 2])
            for half, (wsrc, wdst) in enumerate(((wgq, wgT), (wuq, wuT))):
                for fg in range(2):
                    pba = wA_state["ga"] % 2
                    if wA_state["ga"] >= 2:
                        P.wait("tensor", sS2, wA_ev[wA_state["ga"] - 2])
                    for j in range(4):
                        fn = lambda e, wsrc=wsrc, fg=fg, j=j, pba=pba: e.transpose(
                            pwa[pba][:, j * 128:(j + 1) * 128],
                            wsrc[:, fg * 512 + j * 128: fg * 512 + j * 128 + 128],
                            idb[:, :])
                        if j == 3:
                            emit(P, "tensor", sT2, fn)
                        else:
                            P.t(fn)
                    P.wait("scalar", sT2)
                    wA_ev[wA_state["ga"]] = emit(
                        P, "scalar", sS2,
                        lambda e, wdst=wdst, buf=buf, fg=fg, pba=pba: e.activation(
                            out=wdst[buf][:, fg * 512:(fg + 1) * 512],
                            in_=pwa[pba][:, :], func=AF.Copy, bias=0.0, scale=1.0))
                    wA_state["ga"] += 1
            wT_ready[ic] = sS2.n
            wq_tr_done[ic] = sT2.n

        ffnA_load(0)
        ffnA_load(1)
        ffnA_prep(0)
        ffnA_transpose(0)
        ffnA_prep(1)
        ffnA_transpose(1)
        sttn = {}
        for ic in range(NIC):
            buf = ic % 2
            if ic + 2 < NIC:
                ffnA_load(ic + 2)
            P.wait("tensor", sS2, wT_ready[ic])
            for tt in range(4):
                pb = tt % 2
                gk = ic * 4 + tt
                if gk >= 2:
                    P.wait("tensor", sV2, sttn[gk - 2])
                for kc in range(8):
                    P.t(lambda e, buf=buf, kc=kc, tt=tt, pb=pb: e.matmul(
                        gps[pb][:, :], wgT[buf][:, kc * 128:(kc + 1) * 128],
                        xT[:, kc * TOK + tt * 512: kc * TOK + (tt + 1) * 512],
                        start=(kc == 0), stop=(kc == 7)))
                for kc in range(8):
                    fn = lambda e, buf=buf, kc=kc, tt=tt, pb=pb: e.matmul(
                        ups[pb][:, :], wuT[buf][:, kc * 128:(kc + 1) * 128],
                        xT[:, kc * TOK + tt * 512: kc * TOK + (tt + 1) * 512],
                        start=(kc == 0), stop=(kc == 7))
                    if kc == 7:
                        emit(P, "tensor", sT, fn)
                    else:
                        P.t(fn)
                P.wait("scalar", sT)
                if gk >= 1:
                    P.wait("scalar", sV2, sttn[gk - 1])
                emit(P, "scalar", sS, lambda e, pb=pb, buf=buf: e.activation(
                    out=gtmp[:, :], in_=gps[pb][:, :], func=AF.Silu,
                    scale=sduv[:, 8 + 2 * buf:9 + 2 * buf]))
                P.wait("vector", sS)
                sttn[gk] = emit(P, "vector", sV2,
                                lambda e, ic=ic, tt=tt, pb=pb, buf=buf:
                                e.scalar_tensor_tensor(
                    out=gT[:, ic * TOK + tt * 512: ic * TOK + (tt + 1) * 512],
                    in0=ups[pb][:, :],
                    scalar=sduv[:, 9 + 2 * buf:10 + 2 * buf],
                    in1=gtmp[:, :], op0=ALU.mult, op1=ALU.mult))
                if tt == 3:
                    emit(P, "vector", sV, lambda e, ic=ic: e.tensor_reduce(
                        out=sums[:, 6:7],
                        in_=gT[:, ic * TOK:(ic + 1) * TOK],
                        axis=AX.X, op=ALU.max, apply_absolute_value=True))
                    P.wait("vector", sV)
                    P.v(lambda e: e.tensor_max(hmax[:, 1:2], hmax[:, 1:2],
                                               sums[:, 6:7]))
            sT_after_ic[ic] = sT.n
            if ic >= 1 and ic + 1 < NIC:
                ffnA_transpose(ic + 1)
            if ic + 2 < NIC:
                ffnA_prep(ic + 2)
        P.wait("vector", sV)
        fin_max = emit(P, "vector", sV, lambda e: e.tensor_max(
            hmax[:, 1:2], hmax[:, 1:2], hmax[:, 1:2]))
        P.wait("vector", sV)
        ffnA_done = sV2.n
        P.flush()
        st_pa.close()

        st_p6 = ExitStack()
        scr6 = st_p6.enter_context(nc.sbuf_tensor("scr6", [128, 512], F32))
        x2sb = [st_p6.enter_context(nc.sbuf_tensor(f"x2sb{i}", [128, E], F32))
                for i in range(2)]
        ysb = [st_p6.enter_context(nc.sbuf_tensor(f"ysb{i}", [128, E], F32))
               for i in range(2)]
        dps = [pst(f"dps{i}", [128, 512], st_p6) for i in range(4)]
        ld_of = {}
        for ch0 in range(2):
            ld_of[ch0] = emit(P, "scalar", sX2[ch0], lambda e, ch0=ch0: e.dma_start(
                out=x2sb[ch0][:, :],
                in_=bass.AP(x2_dram, ch0 * 128 * E, [[E, 128], [1, E]])), 16)

        emit(P, "tensor", sT, lambda e: e.transpose(psum_tr[0:1, :], hmax[:, 1:2],
                                                    idf32[:, :]))
        P.wait("vector", sT)
        emit(P, "vector", sV, lambda e: e.tensor_copy(rowbuf[:, :], psum_tr[0:1, :]))
        P.wait("vector", sV)
        emit(P, "vector", sV, lambda e: e.reduce_max(out=smax[0:1, 0:1],
                                                     in_=rowbuf[:, :], axis=AX.X))
        P.wait("vector", sV)
        P.wait("sync", sV)
        argn = emit(P, "sync", sSY, lambda e: e.dma_start(out=arg_i.ap(),
                                                          in_=smax[:, :]), 16)
        P.wait("gpsimd", sSY, argn)
        emit(P, "gpsimd", sCC, lambda e: e.collective_compute(
            "AllReduce", ALU.max, replica_groups=ALLC,
            ins=[arg_i.ap().opt()], outs=[arg_o.ap().opt()]))
        P.wait("gpsimd", sCC)
        mg_ready = emit(P, "gpsimd", sGP, lambda e: e.dma_start(
            out=sc11[0:1, 6:7], in_=bass.AP(arg_o, 0, [[1, 1], [1, 1]])), 16)
        P.wait("gpsimd", sGP, mg_ready)
        P.wait("scalar", sGP, mg_ready)
        emit(P, "scalar", sS, lambda e: e.activation(out=sc11[0:1, 7:8],
                                                     in_=sc11[0:1, 1:2], func=AF.Sigmoid))
        P.wait("vector", sS)
        emit(P, "vector", sV, lambda e: e.tensor_mul(sc11[0:1, 8:9], sc11[0:1, 7:8],
                                                     sc11[0:1, 6:7]))
        P.wait("vector", sV)
        emit(P, "vector", sV, lambda e: e.tensor_scalar(
            out=sc11[0:1, 8:9], in0=sc11[0:1, 8:9],
            scalar1=QINV, scalar2=None, op0=ALU.mult))
        P.wait("vector", sV)
        emit(P, "vector", sV, lambda e: e.reciprocal(out=sc11[0:1, 9:10],
                                                     in_=sc11[0:1, 8:9]))
        P.wait("vector", sV)
        P.wait("tensor", sV)
        for srcc, dstc in ((9, 1), (8, 3)):
            emit(P, "tensor", sT, lambda e, srcc=srcc: e.matmul(
                ps_bc3[:, 0:1], ones_f1[:, :], sc11[0:1, srcc:srcc + 1],
                start=True, stop=True))
            P.wait("vector", sT)
            emit(P, "vector", sV, lambda e, dstc=dstc: e.tensor_copy(
                scal[:, dstc:dstc + 1], ps_bc3[:, 0:1]))
            P.wait("vector", sV)
            P.wait("tensor", sV)

        P.wait("tensor", sV)
        for half in range(2):
            emit(P, "tensor", sT, lambda e, half=half: e.matmul(
                ps_bc3[:, 0:512], ones_f1[:, :], gbrow[0:1, half * 512:(half + 1) * 512],
                start=True, stop=True))
            P.wait("vector", sT)
            emit(P, "vector", sV, lambda e, half=half: e.tensor_copy(
                vfin[:, half * 512:(half + 1) * 512], ps_bc3[:, 0:512]))
            P.wait("tensor", sV)
        for hc in range(8):
            bcast_piece(sduv[:, hc:hc + 1], vtmp[:, :], ps_bc3[:, 0:128])
            P.v(lambda e, hc=hc: e.tensor_mul(
                vfin[:, hc * 128:(hc + 1) * 128], vfin[:, hc * 128:(hc + 1) * 128],
                vtmp[:, :]))
        vfin_done = emit(P, "vector", sV, lambda e: e.tensor_scalar(
            out=vfin[:, :], in0=vfin[:, :], scalar1=scal[:, 3:4], scalar2=None,
            op0=ALU.mult))

        qs_ev = {}
        qb_ev = {}
        qb_after = {}
        qk_state = {"k": 0}

        def quant_tt(tt):
            for ic in range(NIC):
                kq = qk_state["k"]
                h = kq % 2
                if kq >= 2:
                    P.wait("scalar", qb_ev[kq - 2][0], qb_ev[kq - 2][1])
                qs_ev[kq] = emit(P, "scalar", sS, lambda e, ic=ic, tt=tt, h=h:
                                 e.activation(
                    out=i8big[:, h * 512:(h + 1) * 512],
                    in_=gT[:, ic * TOK + tt * 512: ic * TOK + (tt + 1) * 512],
                    func=AF.Copy, bias=0.0, scale=scal[:, 1:2]))
                P.wait("vector", sS, qs_ev[kq])
                bn = emit(P, "vector", sV, lambda e, ic=ic, tt=tt, h=h: e.tensor_copy(
                    gT[:, ic * TOK + tt * 512: ic * TOK + (tt + 1) * 512],
                    i8big[:, h * 512:(h + 1) * 512]))
                qb_ev[kq] = (sV, bn)
                qk_state["k"] += 1
            qb_after[tt] = sV.n

        P.wait("scalar", sV)
        store_of = {}
        ep_of = {}
        quant_tt(0)
        for ch in range(NCH):
            tt = ch // 4
            xb2 = ch % 2
            if ch % 4 == 1 and tt + 1 < 4:
                quant_tt(tt + 1)
            if ch >= 2:
                P.wait("scalar", sV2, ep_of[ch - 2])
                ld_of[ch] = emit(P, "scalar", sX2[xb2], lambda e, ch=ch, xb2=xb2:
                                 e.dma_start(
                    out=x2sb[xb2][:, :],
                    in_=bass.AP(x2_dram, ch * 128 * E, [[E, 128], [1, E]])), 16)
            if ch % 4 == 0:
                P.wait("tensor", sV, qb_after[tt])
            if ch >= 2:
                P.wait("tensor", sV2, ep_of[ch - 2])
            for ic in range(NIC):
                for ec in range(2):
                    fn = lambda e, ch=ch, ec=ec, ic=ic, xb2=xb2: e.matmul(
                        dps[2 * xb2 + ec][:, :],
                        gT[:, ic * TOK + ch * 128: ic * TOK + (ch + 1) * 128],
                        wdT[:, ic * 1024 + ec * 512: ic * 1024 + (ec + 1) * 512],
                        start=(ic == 0), stop=(ic == NIC - 1))
                    if ic == NIC - 1 and ec == 1:
                        emit(P, "tensor", sT, fn)
                    else:
                        P.t(fn)
            P.wait("vector", sT)
            P.wait("vector", sX2[xb2], ld_of[ch])
            if ch >= 2:
                P.wait("vector", sY2[xb2], store_of[ch - 2])
            for ec in range(2):
                P.v(lambda e, ec=ec, xb2=xb2: e.tensor_mul(
                    scr6[:, :], dps[2 * xb2 + ec][:, :],
                    vfin[:, ec * 512:(ec + 1) * 512]))
                ep_of[ch] = emit(P, "vector", sV2,
                                 lambda e, ec=ec, xb2=xb2: e.tensor_add(
                                     ysb[xb2][:, ec * 512:(ec + 1) * 512],
                                     scr6[:, :],
                                     x2sb[xb2][:, ec * 512:(ec + 1) * 512]))
            P.wait("sync", sV2)
            store_of[ch] = emit(P, "sync", sY2[xb2], lambda e, ch=ch, xb2=xb2:
                                e.dma_start(
                out=bass.AP(y_out.tensor, ch * 128 * E, [[E, 128], [1, E]]),
                in_=ysb[xb2][:, :]), 16)
        P.wait("sync", sY2[0], sY2[0].n)
        P.wait("sync", sY2[1], sY2[1].n)
        P.flush()
        st_p6.close()
        st_p5.close()
        st_ffn.close()
    return nc


_NC_CACHE = {}


def kernel(L0_attn_inputs, L0_attn_Wq, L0_attn_Wk, L0_attn_Wvdown, L0_attn_Wvup,
           L0_ffn_clip_x, L0_ffn_clip_down, L0_ffn_w_gate, L0_ffn_w_up,
           L0_ffn_w_down, L0_c, L0_wmod):
    x = np.ascontiguousarray(np.asarray(L0_attn_inputs, dtype=np.float32))
    wq = np.ascontiguousarray(np.asarray(L0_attn_Wq, dtype=np.float32))
    wk = np.ascontiguousarray(np.asarray(L0_attn_Wk, dtype=np.float32))
    wvd = np.ascontiguousarray(np.asarray(L0_attn_Wvdown, dtype=np.float32))
    wvu = np.ascontiguousarray(np.asarray(L0_attn_Wvup, dtype=np.float32))
    cx = np.asarray(L0_ffn_clip_x, dtype=np.float32).reshape(1, 1)
    cd = np.asarray(L0_ffn_clip_down, dtype=np.float32).reshape(1, 1)
    wg = np.ascontiguousarray(np.asarray(L0_ffn_w_gate, dtype=np.float32))
    wu = np.ascontiguousarray(np.asarray(L0_ffn_w_up, dtype=np.float32))
    wd = np.ascontiguousarray(np.asarray(L0_ffn_w_down, dtype=np.float32))
    cc = np.ascontiguousarray(np.asarray(L0_c, dtype=np.float32))
    wmod = np.ascontiguousarray(np.asarray(L0_wmod, dtype=np.float32))

    if "nc" not in _NC_CACHE:
        _NC_CACHE["nc"] = build()
    nc = _NC_CACHE["nc"]

    in_maps = []
    for c in range(8):
        b, j = c // 4, c % 4
        x_own = np.ascontiguousarray(
            np.concatenate([x[b, 512 * g:512 * (g + 1), :] for g in GSTS[j]], axis=0))
        in_maps.append({
            "x_own": x_own,
            "wq": wq, "wk": wk, "wvd": wvd, "wvu": wvu,
            "clip_x": cx, "clip_down": cd,
            "w_gate": wg, "w_up": wu, "w_down": wd,
            "c_b": np.ascontiguousarray(cc[b]),
            "wmod_sl": np.ascontiguousarray(wmod[:, 1536 * j:1536 * (j + 1)]),
        })
    res = run_bass_kernel_spmd(nc, in_maps, core_ids=list(range(8)))
    y = np.empty((B, N, E), dtype=np.float32)
    for c in range(8):
        b, j = c // 4, c % 4
        yo = res.results[c]["y_own"]
        for s, g in enumerate(GSTS[j]):
            y[b, 512 * g:512 * (g + 1), :] = yo[512 * s:512 * (s + 1), :]
    return y


# version 25 (final)
# speedup vs baseline: 1.0282x; 1.0282x over previous
import numpy as np
from contextlib import ExitStack

import concourse.bass as bass
import concourse.mybir as mybir
from concourse.bass_utils import run_bass_kernel_spmd

F32 = mybir.dt.float32
BF16 = mybir.dt.bfloat16
I8 = mybir.dt.int8
AF = mybir.ActivationFunctionType
ALU = mybir.AluOpType
AX = mybir.AxisListType

B, N, E, D, II = 2, 8192, 1024, 256, 2816
TOK = 2048
NCH = 16
NIC = II // 128
EXP_BIAS = -30.0
EXP_SCALE = 0.0625
NEG = -1.0e30
EPS = 1e-5
QINV = 1.0 / 127.0

GSTS = [[j, 7 - j, 8 + j, 15 - j] for j in range(4)]
SLOT_KEY = [GSTS[s // 4][s % 4] for s in range(16)]

ENGS = ("tensor", "vector", "scalar", "sync", "gpsimd")


class Ctr:
    def __init__(self, stack, nc, name):
        self.h = stack.enter_context(nc.semaphore(name))
        self.n = 0


class Prog:
    def __init__(self, nc):
        self.nc = nc
        self.streams = {e: [] for e in ENGS}

    def on(self, eng, fn):
        self.streams[eng].append(fn)

    def t(self, fn): self.on("tensor", fn)
    def v(self, fn): self.on("vector", fn)
    def s(self, fn): self.on("scalar", fn)
    def sy(self, fn): self.on("sync", fn)
    def g(self, fn): self.on("gpsimd", fn)

    def wait(self, eng, ctr, val=None):
        tgt = ctr.n if val is None else val
        self.on(eng, lambda e, h=ctr.h, v=tgt: e.wait_ge(h, v))

    def flush(self):
        active = [e for e in ENGS if self.streams[e]]
        if not active:
            return
        with self.nc.Block() as block:
            for e in active:
                fns = self.streams[e]

                def runner(eng, fns=fns):
                    for f in fns:
                        f(eng)
                getattr(block, e)(runner)
        self.streams = {e: [] for e in ENGS}


def emit(P, eng, ctr, fn, k=1):
    if ctr is None:
        P.on(eng, fn)
        return None
    ctr.n += k

    def wrapped(e, fn=fn, h=ctr.h, k=k):
        fn(e).then_inc(h, k)
    P.on(eng, wrapped)
    return ctr.n


def build():
    nc = bass.Bass("TRN2", target_bir_lowering=False, detect_race_conditions=False)

    x_in = nc.dram_tensor("x_own", [TOK, E], F32, kind="ExternalInput").ap()
    wq_in = nc.dram_tensor("wq", [E, D], F32, kind="ExternalInput").ap()
    wk_in = nc.dram_tensor("wk", [E, D], F32, kind="ExternalInput").ap()
    wvd_in = nc.dram_tensor("wvd", [E, D], F32, kind="ExternalInput").ap()
    wvu_in = nc.dram_tensor("wvu", [D, E], F32, kind="ExternalInput").ap()
    cx_in = nc.dram_tensor("clip_x", [1, 1], F32, kind="ExternalInput").ap()
    cd_in = nc.dram_tensor("clip_down", [1, 1], F32, kind="ExternalInput").ap()
    wg_in = nc.dram_tensor("w_gate", [II, E], F32, kind="ExternalInput").ap()
    wu_in = nc.dram_tensor("w_up", [II, E], F32, kind="ExternalInput").ap()
    wd_in = nc.dram_tensor("w_down", [E, II], F32, kind="ExternalInput").ap()
    c_in = nc.dram_tensor("c_b", [E], F32, kind="ExternalInput").ap()
    wmod_in = nc.dram_tensor("wmod_sl", [E, 1536], F32, kind="ExternalInput").ap()
    y_out = nc.dram_tensor("y_own", [TOK, E], F32, kind="ExternalOutput").ap()

    mod_ag_i = nc.dram_tensor("mod_ag_i", [1, 1536], F32)
    mod_ag_o = nc.dram_tensor("mod_ag_o", [4, 1536], F32)
    agk_i = nc.dram_tensor("agk_i", [2, 128, TOK], BF16)
    agk_o = nc.dram_tensor("agk_o", [8, 128, TOK], BF16)
    agv_i = nc.dram_tensor("agv_i", [16, 128, D], BF16)
    agv_o = nc.dram_tensor("agv_o", [64, 128, D], BF16)
    arx_i = nc.dram_tensor("arx_i", [1, 8], F32)
    arx_o = nc.dram_tensor("arx_o", [1, 8], F32)
    arg_i = nc.dram_tensor("arg_i", [1, 8], F32)
    arg_o = nc.dram_tensor("arg_o", [1, 8], F32)
    x2_dram = nc.dram_tensor("x2_spill", [TOK, E], F32)

    GRP = [[0, 1, 2, 3], [4, 5, 6, 7]]
    ALLC = [[0, 1, 2, 3, 4, 5, 6, 7]]

    st0 = ExitStack()
    with st0:
        def sb(name, shape, dt, stk=st0):
            return stk.enter_context(nc.sbuf_tensor(name, shape, dt))

        def pst(name, shape, stk):
            return stk.enter_context(nc.psum_tensor(name, shape, F32))

        consts = sb("consts", [128, 2], F32)
        idf32 = sb("idf32", [128, 128], F32)
        idb = sb("idb", [128, 128], BF16)
        ones_b = sb("ones_b", [128, 1], BF16)
        ones_f1 = sb("ones_f1", [1, 128], F32)
        xT = sb("xT", [128, 8 * TOK], BF16)
        stats = sb("stats", [128, 96], F32)
        sums = sb("sums", [128, 32], F32)
        mod_pp = sb("mod_pp", [128, 48], F32)
        ca1_pp = sb("ca1_pp", [128, 8], F32)
        uvvec = sb("uvvec", [128, 12], F32)
        scal = sb("scal", [128, 4], F32)
        sc11 = sb("sc11", [1, 12], F32)
        smax = sb("smax", [1, 8], F32)
        hmax = sb("hmax", [128, 2], F32)
        rowbuf = sb("rowbuf", [1, 128], F32)
        gbrow = sb("gbrow", [1, 1024], F32)
        i8big = sb("i8big", [128, E], I8)

        sSY = Ctr(st0, nc, "sSY")
        sSC = Ctr(st0, nc, "sSC")
        sGP = Ctr(st0, nc, "sGP")
        sT = Ctr(st0, nc, "sT")
        sT2 = Ctr(st0, nc, "sT2")
        sV = Ctr(st0, nc, "sV")
        sV2 = Ctr(st0, nc, "sV2")
        sS = Ctr(st0, nc, "sS")
        sS2 = Ctr(st0, nc, "sS2")
        sCC = Ctr(st0, nc, "sCC")
        sGB = Ctr(st0, nc, "sGB")
        sWQ = Ctr(st0, nc, "sWQ")
        sY2 = [Ctr(st0, nc, f"sY2_{i}") for i in range(2)]
        sSP = Ctr(st0, nc, "sSP")
        sX2 = [Ctr(st0, nc, f"sX2_{i}") for i in range(2)]

        nc.const_aps.register(F32, EXP_BIAS, consts[:, 0:1])
        nc.const_aps.register(F32, EXP_SCALE, consts[:, 1:2])

        psum_tr = pst("psum_tr", [128, 128], st0)

        st_x = ExitStack()
        x_res = st_x.enter_context(nc.sbuf_tensor("x_res", [128, NCH * E], F32, side="right"))

        st_att = ExitStack()
        QT = st_att.enter_context(nc.sbuf_tensor("QT", [128, 2 * TOK], BF16))
        wvu_ga = st_att.enter_context(nc.sbuf_tensor("wvu_ga", [128, 2 * E], BF16))

        P = Prog(nc)

        def bcast_piece(colap, dst_slice, bank_slice):
            emit(P, "tensor", sT, lambda e, colap=colap: e.transpose(
                psum_tr[0:1, :], colap, idf32[:, :]))
            P.wait("vector", sT)
            emit(P, "vector", sV, lambda e: e.tensor_copy(rowbuf[:, :],
                                                          psum_tr[0:1, :]))
            P.wait("tensor", sV)
            emit(P, "tensor", sT, lambda e, bank_slice=bank_slice: e.matmul(
                bank_slice, ones_f1[:, :], rowbuf[:, :], start=True, stop=True))
            P.wait("vector", sT)
            emit(P, "vector", sV, lambda e, dst_slice=dst_slice, bank_slice=bank_slice:
                 e.tensor_copy(dst_slice, bank_slice))
            P.wait("vector", sV)
            P.wait("tensor", sV)

        p2 = ExitStack()
        scratch = sb("scratch", [128, E], F32, p2)
        wq_b = sb("wq_b", [128, 8 * D], BF16, p2)
        wk_b = sb("wk_b", [128, 8 * D], BF16, p2)
        wvd_b = sb("wvd_b", [128, 8 * D], BF16, p2)
        wvu_b = sb("wvu_b", [128, 2 * E], BF16, p2)
        cvec = sb("cvec", [128, 8], F32, p2)
        ctmp = sb("ctmp", [8, 128], F32, p2)
        modT = sb("modT", [48, 128], F32, p2)
        cvec_b = sb("cvec_b", [128, 8], BF16, p2)
        wmod_bf = [sb(f"wmodbf{i}", [128, 2 * 1536], BF16, p2) for i in range(2)]
        modsl = sb("modsl", [1, 1536], F32, p2)
        kst = sb("kst", [128, 2 * TOK], BF16, p2)
        vst = sb("vst", [128, 16 * D], BF16, p2)
        m_s = sb("m_s", [128, 512], F32, p2)
        r_s = sb("r_s", [128, 512], F32, p2)
        ga_bc = sb("ga_bc", [128, E], F32, p2)
        uvv_bc = sb("uvv_bc", [128, 2 * D], F32, p2)
        rowA = sb("rowA", [1, 1024], F32, p2)
        rowB = sb("rowB", [1, 1024], F32, p2)
        rowC = sb("rowC", [1, 1024], F32, p2)
        uvrow = sb("uvrow", [1, 512], F32, p2)
        ca1c = [sb(f"ca1c{k}", [128, 1], BF16, p2) for k in range(8)]
        sac = [sb(f"sac{k}", [128, 1], BF16, p2) for k in range(8)]
        ps_proj = pst("ps_proj", [128, 512], p2)
        ps_v = pst("ps_v", [128, D], p2)
        ps_bc = pst("ps_bc", [128, 512], p2)
        ps_uv = pst("ps_uv", [1, 512], p2)
        st_mod = ExitStack()
        ps_mod = [pst(f"ps_mod{i}", [1, 512], st_mod) for i in range(3)]

        x_loaded = emit(P, "sync", sSY, lambda e: e.dma_start(
            out=x_res[:, :],
            in_=bass.AP(x_in.tensor, 0, [[E, 128], [128 * E, NCH], [1, E]])), 16)
        P.wait("sync", sSY, x_loaded)
        emit(P, "gpsimd", sWQ, lambda e: e.dma_start(
            out=wq_b[:, :],
            in_=bass.AP(wq_in.tensor, 0, [[D, 128], [128 * D, 8], [1, D]])), 16)
        emit(P, "gpsimd", sWQ, lambda e: e.dma_start(
            out=wk_b[:, :],
            in_=bass.AP(wk_in.tensor, 0, [[D, 128], [128 * D, 8], [1, D]])), 16)
        emit(P, "gpsimd", sWQ, lambda e: e.dma_start(
            out=wvd_b[:, :],
            in_=bass.AP(wvd_in.tensor, 0, [[D, 128], [128 * D, 8], [1, D]])), 16)
        pw_loaded = emit(P, "gpsimd", sWQ, lambda e: e.dma_start(
            out=wvu_b[:, :],
            in_=bass.AP(wvu_in.tensor, 0, [[E, 128], [128 * E, 2], [1, E]])), 16)
        c_ld = emit(P, "scalar", sSC, lambda e: e.dma_start(
            out=ctmp[:, :], in_=bass.AP(c_in.tensor, 0, [[128, 8], [1, 128]])), 16)
        emit(P, "scalar", sSC, lambda e: e.dma_start(out=sc11[0:1, 0:1], in_=cx_in[:, :]), 16)
        clips_ld = emit(P, "scalar", sSC, lambda e: e.dma_start(
            out=sc11[0:1, 1:2], in_=cd_in[:, :]), 16)
        for fn in (lambda e: e.memset(consts[:, 0:1], EXP_BIAS),
                   lambda e: e.memset(consts[:, 1:2], EXP_SCALE),
                   lambda e: e.memset(ones_b[:, :], 1.0),
                   lambda e: e.memset(ones_f1[:, :], 1.0),
                   lambda e: e.memset(idf32[:, :], 1.0),
                   lambda e: e.memset(smax[:, :], 0.0),
                   lambda e: e.memset(hmax[:, :], 0.0)):
            P.v(fn)
        P.flush()

        id_done = emit(P, "gpsimd", sGB, lambda e: e.affine_select(
            out=idf32[:, :], in_=idf32[:, :], pattern=[[1, 128]],
            compare_op=ALU.is_equal, fill=0.0, base=0, channel_multiplier=-1), 1)
        P.wait("vector", sGB, id_done)
        emit(P, "vector", sV, lambda e: e.tensor_copy(idb[:, :], idf32[:, :]))

        P.wait("tensor", sSC, clips_ld)
        P.wait("tensor", sGB, id_done)
        emit(P, "tensor", sT, lambda e: e.transpose(
            psum_tr[:, 0:8], ctmp[:, :], idf32[0:8, 0:8]))
        P.wait("vector", sT)
        cv_done = emit(P, "vector", sV, lambda e: e.tensor_copy(cvec[:, :],
                                                                psum_tr[:, 0:8]))
        P.wait("scalar", sV, cv_done)
        silu_c = emit(P, "scalar", sS, lambda e: e.activation(
            out=cvec_b[:, :], in_=cvec[:, :], func=AF.Silu))

        P.wait("gpsimd", sWQ, pw_loaded)
        P.wait("tensor", sS, silu_c)
        wm_ld = {}
        for g in range(4):
            if g >= 2:
                P.wait("gpsimd", sT)
            wm_ld[g] = emit(P, "gpsimd", sGP, lambda e, g=g: e.dma_start(
                out=wmod_bf[g % 2][:, :],
                in_=bass.AP(wmod_in.tensor, 2 * g * 128 * 1536,
                            [[1536, 128], [128 * 1536, 2], [1, 1536]])), 16)
            P.wait("gpsimd", sGP, wm_ld[g])
            P.wait("tensor", sGP, wm_ld[g])
            for half in range(2):
                kc = 2 * g + half
                for n3 in range(3):
                    fn = lambda e, g=g, half=half, kc=kc, n3=n3: e.matmul(
                        ps_mod[n3][:, :], cvec_b[:, kc:kc + 1],
                        wmod_bf[g % 2][:, half * 1536 + n3 * 512:
                                       half * 1536 + (n3 + 1) * 512],
                        start=(kc == 0), stop=(kc == 7))
                    if n3 == 2 and half == 1:
                        emit(P, "tensor", sT, fn)
                    else:
                        P.t(fn)
        P.wait("scalar", sT)
        for n3 in range(3):
            fn = lambda e, n3=n3: e.copy(modsl[0:1, n3 * 512:(n3 + 1) * 512],
                                         ps_mod[n3][:, :])
            emit(P, "scalar", sS2, fn)
        P.wait("sync", sS2)
        agn = emit(P, "sync", sSY, lambda e: e.dma_start(
            out=mod_ag_i.ap(), in_=modsl[:, :]), 16)
        P.wait("gpsimd", sSY, agn)
        emit(P, "gpsimd", sCC, lambda e: e.collective_compute(
            "AllGather", ALU.bypass, replica_groups=GRP,
            ins=[mod_ag_i.ap().opt()], outs=[mod_ag_o.ap().opt()]))
        P.wait("gpsimd", sCC)
        modT_n = emit(P, "gpsimd", sGP, lambda e: e.dma_start(
            out=modT[:, :], in_=bass.AP(mod_ag_o, 0, [[128, 48], [1, 128]])), 16)
        P.wait("gpsimd", sGP, modT_n)
        ga_row_n = emit(P, "gpsimd", sGP, lambda e: e.dma_start(
            out=rowA[:, :], in_=bass.AP(mod_ag_o, 2 * 1024, [[1, 1], [1, 1024]])), 16)
        P.wait("gpsimd", sGP, ga_row_n)
        P.wait("tensor", sGP, modT_n)
        emit(P, "tensor", sT, lambda e: e.transpose(
            psum_tr[:, 0:48], modT[:, :], idf32[0:48, 0:48]))
        P.wait("vector", sT)
        mod_ready_v = emit(P, "vector", sV, lambda e: e.tensor_copy(
            mod_pp[:, :], psum_tr[:, 0:48]))

        P.wait("vector", sV, mod_ready_v)
        emit(P, "vector", sV, lambda e: e.tensor_scalar(
            out=ca1_pp[:, :], in0=mod_pp[:, 8:16],
            scalar1=1.0, scalar2=None, op0=ALU.add))
        P.wait("vector", sV)
        for k in range(8):
            P.v(lambda e, k=k: e.tensor_copy(ca1c[k][:, :], ca1_pp[:, k:k + 1]))
            fnc = lambda e, k=k: e.tensor_copy(sac[k][:, :], mod_pp[:, k:k + 1])
            if k == 7:
                uvr_ready = emit(P, "vector", sV, fnc)
            else:
                P.v(fnc)

        P.flush()

        P.wait("vector", sSY, x_loaded)
        P.wait("scalar", sSY, x_loaded)
        P.wait("gpsimd", sSY, x_loaded)
        for ch in range(NCH):
            fnr = lambda e, ch=ch: e.reduce_sum(
                out=sums[:, ch:ch + 1], in_=x_res[:, ch * E:(ch + 1) * E], axis=AX.X)
            if ch == NCH - 1:
                emit(P, "vector", sV, fnr)
            else:
                P.v(fnr)
            emit(P, "scalar", sS, lambda e, ch=ch: e.activation(
                out=scratch[:, :], in_=x_res[:, ch * E:(ch + 1) * E],
                func=AF.Square, accum_out=sums[:, 16 + ch:17 + ch]))
        sq_done = sS.n
        P.wait("vector", sV)
        emit(P, "vector", sV, lambda e: e.tensor_scalar(
            out=stats[:, 0:16], in0=sums[:, 0:16],
            scalar1=1.0 / E, scalar2=None, op0=ALU.mult))
        P.wait("vector", sV)
        P.wait("vector", sS, sq_done)
        emit(P, "vector", sV, lambda e: e.tensor_scalar(
            out=sums[:, 16:32], in0=sums[:, 16:32],
            scalar1=1.0 / E, scalar2=None, op0=ALU.mult))
        P.wait("vector", sV)
        emit(P, "vector", sV, lambda e: e.tensor_mul(stats[:, 80:96], stats[:, 0:16], stats[:, 0:16]))
        P.wait("vector", sV)
        emit(P, "vector", sV, lambda e: e.tensor_sub(stats[:, 16:32], sums[:, 16:32],
                                                     stats[:, 80:96]))
        P.wait("vector", sV)
        emit(P, "vector", sV, lambda e: e.tensor_scalar(
            out=stats[:, 16:32], in0=stats[:, 16:32], scalar1=EPS, scalar2=None, op0=ALU.add))
        P.wait("vector", sV)
        emit(P, "vector", sV, lambda e: e.reciprocal(out=stats[:, 80:96],
                                                     in_=stats[:, 16:32]))
        P.wait("vector", sV)
        P.wait("scalar", sV)
        r1_done = emit(P, "scalar", sS, lambda e: e.activation(
            out=stats[:, 16:32], in_=stats[:, 80:96], func=AF.Sqrt))

        ptr1 = [ps_proj, ps_bc]
        P.wait("tensor", sSY, x_loaded)
        evac_of1 = {}
        g1 = 0
        for s in range(4):
            for fc in range(8):
                pb = g1 % 2
                if g1 >= 2:
                    P.wait("tensor", sS2, evac_of1[g1 - 2])
                for cc in range(4):
                    fn = lambda e, s=s, fc=fc, cc=cc, pb=pb: e.transpose(
                        ptr1[pb][:, cc * 128:(cc + 1) * 128],
                        x_res[:, (s * 4 + cc) * E + fc * 128:
                              (s * 4 + cc) * E + fc * 128 + 128],
                        idf32[:, :])
                    if cc == 3:
                        emit(P, "tensor", sT2, fn)
                    else:
                        P.t(fn)
                P.wait("scalar", sT2)
                evac_of1[g1] = emit(P, "scalar", sS2, lambda e, s=s, fc=fc, pb=pb:
                                    e.activation(
                                        out=xT[:, fc * TOK + s * 512:
                                               fc * TOK + s * 512 + 512],
                                        in_=ptr1[pb][:, :], func=AF.Copy,
                                        bias=0.0, scale=1.0))
                g1 += 1
        xT_raw = sS2.n
        P.wait("tensor", sS2, xT_raw)

        P.wait("vector", sS2, xT_raw)
        for fc in range(8):
            emit(P, "vector", sV, lambda e, fc=fc: e.tensor_scalar(
                out=xT[:, fc * TOK:(fc + 1) * TOK], in0=xT[:, fc * TOK:(fc + 1) * TOK],
                scalar1=ca1_pp[:, fc:fc + 1], scalar2=None, op0=ALU.mult))
        xhat_done = sV.n

        P.wait("vector", sGP, ga_row_n)
        for fn in (lambda e: e.tensor_scalar(out=rowB[:, :], in0=rowA[:, :],
                                             scalar1=0.0, scalar2=None, op0=ALU.is_ge),
                   lambda e: e.tensor_scalar(out=rowB[:, :], in0=rowB[:, :],
                                             scalar1=2.0, scalar2=-1.0,
                                             op0=ALU.mult, op1=ALU.add),
                   lambda e: e.tensor_scalar(out=rowC[:, :], in0=rowA[:, :],
                                             scalar1=-1.0, scalar2=None, op0=ALU.mult),
                   lambda e: e.tensor_max(rowC[:, :], rowC[:, :], rowA[:, :]),
                   lambda e: e.tensor_scalar(out=rowC[:, :], in0=rowC[:, :],
                                             scalar1=0.25, scalar2=None, op0=ALU.max)):
            emit(P, "vector", sV, fn)
            P.wait("vector", sV)
        ga_done = emit(P, "vector", sV, lambda e: e.tensor_mul(
            rowA[:, :], rowC[:, :], rowB[:, :]))
        P.wait("tensor", sV, ga_done)
        for half in range(2):
            emit(P, "tensor", sT, lambda e, half=half: e.matmul(
                ps_bc[:, 0:512], ones_f1[:, :], rowA[0:1, half * 512:(half + 1) * 512],
                start=True, stop=True))
            P.wait("vector", sT)
            emit(P, "vector", sV, lambda e, half=half: e.tensor_copy(
                ga_bc[:, half * 512:(half + 1) * 512], ps_bc[:, 0:512]))
            P.wait("tensor", sV)
        P.wait("vector", sWQ, pw_loaded)
        for dv in range(2):
            P.v(lambda e, dv=dv: e.tensor_copy(scratch[:, :],
                                               wvu_b[:, dv * E:(dv + 1) * E]))
            emit(P, "vector", sV, lambda e, dv=dv: e.tensor_mul(
                wvu_ga[:, dv * E:(dv + 1) * E], scratch[:, :], ga_bc[:, :]))
        P.flush()

        st_mod.close()
        ps_proj2 = pst("ps_proj2", [128, 512], p2)
        ps_v2 = pst("ps_v2", [128, D], p2)
        ps_pj = [ps_proj, ps_proj2]
        ps_vv = [ps_v, ps_v2]
        P.wait("tensor", sV, uvr_ready)
        P.wait("tensor", sWQ, pw_loaded)
        for wi, wbuf in enumerate((wk_b, wq_b, wvd_b)):
            for half, cols in ((0, ca1c), (1, sac)):
                for kc in range(8):
                    fn = lambda e, wbuf=wbuf, half=half, cols=cols, kc=kc: e.matmul(
                        ps_uv[0:1, half * 256:(half + 1) * 256],
                        cols[kc][:, :],
                        wbuf[:, kc * D:(kc + 1) * D],
                        start=(kc == 0), stop=(kc == 7))
                    if kc == 7:
                        emit(P, "tensor", sT, fn)
                    else:
                        P.t(fn)
            P.wait("vector", sT)
            emit(P, "vector", sV, lambda e: e.tensor_copy(uvrow[:, :], ps_uv[0:1, :]))
            P.wait("tensor", sV)
            if wi < 2:
                for uv in range(2):
                    for dc in range(2):
                        emit(P, "tensor", sT, lambda e, uv=uv, dc=dc: e.transpose(
                            psum_tr[:, 0:1],
                            uvrow[0:1, uv * 256 + dc * 128: uv * 256 + dc * 128 + 128],
                            idf32[0:1, 0:1]))
                        P.wait("vector", sT)
                        emit(P, "vector", sV, lambda e, wi=wi, uv=uv, dc=dc:
                             e.tensor_copy(
                                 uvvec[:, 4 * wi + 2 * dc + uv:
                                       4 * wi + 2 * dc + uv + 1],
                                 psum_tr[:, 0:1]))
                        P.wait("vector", sV)
                        P.wait("tensor", sV)
            else:
                for uv in range(2):
                    emit(P, "tensor", sT, lambda e, uv=uv: e.matmul(
                        ps_bc[:, 0:256], ones_f1[:, :],
                        uvrow[0:1, uv * 256:(uv + 1) * 256], start=True, stop=True))
                    P.wait("vector", sT)
                    emit(P, "vector", sV, lambda e, uv=uv: e.tensor_copy(
                        uvv_bc[:, uv * 256:(uv + 1) * 256], ps_bc[:, 0:256]))
                    P.wait("tensor", sV)

        pj_state = {"n": 0}
        pj_ev = {}
        vd_ev = {}
        P.wait("vector", sS, r1_done)
        P.wait("tensor", sV, xhat_done)
        for mc in range(NCH):
            if mc >= 2:
                P.wait("tensor", sV, vd_ev[mc - 2])
            for kc in range(8):
                fn = lambda e, mc=mc, kc=kc: e.matmul(
                    ps_vv[mc % 2][:, :],
                    xT[:, kc * TOK + mc * 128: kc * TOK + (mc + 1) * 128],
                    wvd_b[:, kc * D:(kc + 1) * D],
                    start=(kc == 0), stop=(kc == 7))
                if kc == 7:
                    emit(P, "tensor", sT, fn)
                else:
                    P.t(fn)
            P.wait("vector", sT)
            vd_ev[mc] = emit(P, "vector", sV, lambda e, mc=mc: e.scalar_tensor_tensor(
                out=scratch[:, 0:D], in0=uvv_bc[:, 0:D],
                scalar=stats[:, mc:mc + 1], in1=ps_vv[mc % 2][:, :],
                op0=ALU.mult, op1=ALU.subtract))
            P.v(lambda e, mc=mc: e.tensor_scalar(
                out=scratch[:, 0:D], in0=scratch[:, 0:D],
                scalar1=stats[:, 16 + mc:17 + mc], scalar2=-1.0,
                op0=ALU.mult, op1=ALU.mult))
            emit(P, "vector", sV, lambda e, mc=mc: e.tensor_add(
                vst[:, mc * D:(mc + 1) * D], scratch[:, 0:D], uvv_bc[:, D:2 * D]))
        vd_done = sV.n
        P.wait("sync", sV, vd_done)
        bounce_v = emit(P, "sync", sSY, lambda e: e.dma_start(
            out=bass.AP(agv_i, 0, [[D, 128], [128 * D, 16], [1, D]]),
            in_=vst[:, :]), 16)
        P.wait("sync", sSY, bounce_v)
        P.wait("gpsimd", sSY, bounce_v)
        agv_done = emit(P, "gpsimd", sCC, lambda e: e.collective_compute(
            "AllGather", ALU.bypass, replica_groups=GRP,
            ins=[agv_i.ap().opt()], outs=[agv_o.ap().opt()]))

        P.wait("tensor", sS, r1_done)
        P.wait("tensor", sV, xhat_done)
        for nt in range(4):
            for half in range(2):
                dstt = (m_s, r_s)[half]
                for c4 in range(4):
                    c = nt * 4 + c4
                    bcast_piece(stats[:, 16 * half + c:16 * half + c + 1],
                                dstt[:, c4 * 128:(c4 + 1) * 128],
                                ps_bc[:, c4 * 128:(c4 + 1) * 128])
            for wi, wbuf in ((0, wk_b), (1, wq_b)):
                base_u = 0 if wi == 0 else 4
                for dc in range(2):
                    pj = pj_state["n"]
                    pjb = pj % 2
                    if pj >= 2:
                        P.wait("tensor", sV, pj_ev[pj - 2])
                    for kc in range(8):
                        fn = lambda e, wbuf=wbuf, dc=dc, nt=nt, kc=kc, pjb=pjb: \
                            e.matmul(
                            ps_pj[pjb][:, :],
                            wbuf[:, kc * D + dc * 128: kc * D + dc * 128 + 128],
                            xT[:, kc * TOK + nt * 512: kc * TOK + (nt + 1) * 512],
                            start=(kc == 0), stop=(kc == 7))
                        if kc == 7:
                            emit(P, "tensor", sT, fn)
                        else:
                            P.t(fn)
                    P.wait("vector", sT)
                    pj_ev[pj] = emit(P, "vector", sV,
                                     lambda e, base_u=base_u, dc=dc, pjb=pjb:
                                     e.scalar_tensor_tensor(
                        out=scratch[:, 0:512], in0=m_s[:, :],
                        scalar=uvvec[:, base_u + 2 * dc:base_u + 2 * dc + 1],
                        in1=ps_pj[pjb][:, :], op0=ALU.mult, op1=ALU.subtract))
                    P.v(lambda e: e.tensor_mul(scratch[:, 0:512], scratch[:, 0:512],
                                               r_s[:, :]))
                    dsl = (kst[:, dc * TOK + nt * 512: dc * TOK + (nt + 1) * 512]
                           if wi == 0 else
                           QT[:, dc * TOK + nt * 512: dc * TOK + (nt + 1) * 512])
                    emit(P, "vector", sV,
                         lambda e, dsl=dsl, base_u=base_u, dc=dc: e.tensor_scalar(
                             out=dsl, in0=scratch[:, 0:512], scalar1=-1.0,
                             scalar2=uvvec[:, base_u + 2 * dc + 1:base_u + 2 * dc + 2],
                             op0=ALU.mult, op1=ALU.add))
                    pj_state["n"] += 1
        proj_done = sV.n
        P.flush()

        for dc in range(2):
            emit(P, "sync", sSY, lambda e, dc=dc: e.dma_start(
                out=bass.AP(agk_i, dc * 128 * TOK, [[TOK, 128], [1, TOK]]),
                in_=kst[:, dc * TOK:(dc + 1) * TOK]), 16)
        bounce_n = sSY.n
        P.wait("sync", sSY, bounce_n)
        P.flush()
        p2.close()

        st_kv = ExitStack()
        KT = st_kv.enter_context(nc.sbuf_tensor("KT", [128, 2 * N], BF16))
        VdT = st_kv.enter_context(nc.sbuf_tensor("VdT", [128, 64 * D], BF16))
        masks = st_kv.enter_context(nc.sbuf_tensor("masks", [128, 4 * 512], F32))
        O_sb = st_kv.enter_context(nc.sbuf_tensor("O_sb", [128, 8 * 512], BF16))
        PT = [st_kv.enter_context(nc.sbuf_tensor(f"PT{i}", [128, 512], BF16))
              for i in range(2)]
        rl_tmp = st_kv.enter_context(nc.sbuf_tensor("rl_tmp", [1, 512], F32))

        P.v(lambda e: e.memset(masks[:, :], 0.0))
        P.flush()
        for ph in range(4):
            P.g(lambda e, ph=ph: e.affine_select(
                out=masks[:, ph * 512:(ph + 1) * 512],
                in_=masks[:, ph * 512:(ph + 1) * 512],
                pattern=[[1, 512]], compare_op=ALU.is_ge,
                fill=NEG, base=-128 * ph, channel_multiplier=-1))
        agk_done = emit(P, "gpsimd", sCC, lambda e: e.collective_compute(
            "AllGather", ALU.bypass, replica_groups=GRP,
            ins=[agk_i.ap().opt()], outs=[agk_o.ap().opt()]))
        P.wait("gpsimd", sCC, agv_done)
        kv_ready = emit(P, "gpsimd", sWQ, lambda e: e.dma_start(
            out=VdT[:, :],
            in_=bass.AP(agv_o, 0, [[D, 128], [128 * D, 64], [1, D]])), 16)
        P.wait("gpsimd", sCC, agk_done)
        for dc in range(2):
            emit(P, "gpsimd", sGP, lambda e, dc=dc: e.dma_start(
                out=KT[:, dc * N:(dc + 1) * N],
                in_=bass.AP(agk_o, dc * 128 * TOK,
                            [[TOK, 128], [2 * 128 * TOK, 4], [1, TOK]])), 16)
        kt_ready = sGP.n
        P.wait("gpsimd", sGP, kt_ready)
        P.flush()

        st_p3 = ExitStack()
        S2 = [pst(f"S2_{i}", [128, 512], st_p3) for i in range(2)]
        O2 = [pst(f"O2_{i}", [128, 512], st_p3) for i in range(2)]
        lps = pst("lps", [1, 512], st_p3)
        atp = [pst(f"atp{i}", [128, 512], st_p3) for i in range(2)]

        base = {"T": sT.n, "T2": sT2.n, "V": sV.n, "S": sS.n}

        def attn_body(j):
            t_l, v_l, s_l = [], [], []
            cnt = {"T": 0, "T2": 0, "V": 0, "S": 0}
            sinc_at = {}
            einc_at = {}
            linc_at = {}
            evac_after = {}
            sched = []
            for qs in range(4):
                gst = GSTS[j][qs]
                full = [sl for sl in range(16) if SLOT_KEY[sl] < gst]
                own = 4 * j + qs
                assert SLOT_KEY[own] == gst
                items = [(sl * 4 + cc, None) for sl in full for cc in range(4)]
                items += [(own * 4 + cc, cc) for cc in range(4)]
                for idx, (kc, mph) in enumerate(items):
                    sched.append((qs, kc, mph, idx == 0, idx == len(items) - 1))
            assert len(sched) == 136

            def emit_sblk(i):
                qs, kc, mph, first, last = sched[i]
                bank = i % 2
                if i >= 2:
                    t_l.append(("wait", sS.h, base["S"] + einc_at[i - 2]))
                for dc in range(2):
                    def mk(e, dc=dc, kc=kc, qs=qs, bank=bank):
                        return e.matmul(
                            S2[bank][:, :],
                            KT[:, dc * N + kc * 128: dc * N + (kc + 1) * 128],
                            QT[:, dc * TOK + qs * 512: dc * TOK + (qs + 1) * 512],
                            start=(dc == 0), stop=(dc == 1))
                    if dc == 1:
                        cnt["T"] += 1
                        t_l.append(("inc", sT.h, mk, 1))
                    else:
                        t_l.append(("do", mk))
                sinc_at[i] = cnt["T"]
                if mph is not None:
                    v_l.append(("wait", sT.h, base["T"] + sinc_at[i]))
                    def mkm(e, bank=bank, mph=mph):
                        return e.tensor_add(S2[bank][:, :], S2[bank][:, :],
                                            masks[:, mph * 512:(mph + 1) * 512])
                    cnt["V"] += 1
                    v_l.append(("inc", sV.h, mkm, 1))
                    s_l.append(("wait", sV.h, base["V"] + cnt["V"]))
                else:
                    s_l.append(("wait", sT.h, base["T"] + sinc_at[i]))
                if i >= 2:
                    s_l.append(("wait", sT2.h, base["T2"] + linc_at[i - 2]))
                def mke(e, bank=bank):
                    return e.activation(out=PT[bank][:, :], in_=S2[bank][:, :],
                                        func=AF.Exp, bias=EXP_BIAS, scale=EXP_SCALE)
                cnt["S"] += 1
                s_l.append(("inc", sS.h, mke, 1))
                einc_at[i] = cnt["S"]

            def emit_oblk(i):
                qs, kc, mph, first, last = sched[i]
                bank = i % 2
                if i == 0:
                    t_l.append(("wait", sWQ.h, kv_ready))
                t_l.append(("wait", sS.h, base["S"] + einc_at[i]))
                if first and qs > 0:
                    t_l.append(("wait", sV.h, base["V"] + evac_after[qs - 1]))
                for dvc in range(2):
                    def mko(e, dvc=dvc, kc=kc, bank=bank, first=first, last=last):
                        return e.matmul(
                            O2[dvc][:, :],
                            VdT[:, kc * D + dvc * 128: kc * D + dvc * 128 + 128],
                            PT[bank][:, :], start=first, stop=last)
                    t_l.append(("do", mko))
                def mkl(e, bank=bank, first=first, last=last):
                    return e.matmul(lps[:, :], ones_b[:, :], PT[bank][:, :],
                                    start=first, stop=last)
                cnt["T2"] += 1
                t_l.append(("inc", sT2.h, mkl, 1))
                linc_at[i] = cnt["T2"]
                if last:
                    v_l.append(("wait", sT2.h, base["T2"] + linc_at[i]))
                    for dvc in range(2):
                        def mkev(e, qs=qs, dvc=dvc):
                            return e.tensor_copy(
                                O_sb[:, qs * 1024 + dvc * 512:
                                     qs * 1024 + (dvc + 1) * 512], O2[dvc][:, :])
                        v_l.append(("do", mkev))
                    def mkl1(e, qs=qs):
                        return e.tensor_scalar(out=rl_tmp[0:1, :],
                                               in0=lps[:, :],
                                               scalar1=1e-30, scalar2=None, op0=ALU.max)
                    cnt["V"] += 1
                    v_l.append(("inc", sV.h, mkl1, 1))
                    v_l.append(("wait", sV.h, base["V"] + cnt["V"]))
                    def mkl2(e, qs=qs):
                        return e.reciprocal(out=rl_tmp[0:1, :], in_=rl_tmp[0:1, :])
                    cnt["V"] += 1
                    v_l.append(("inc", sV.h, mkl2, 1))
                    v_l.append(("wait", sV.h, base["V"] + cnt["V"]))
                    for cc in range(4):
                        t_l.append(("wait", sV.h, base["V"] + cnt["V"]))
                        def mtr(e, cc=cc):
                            return e.transpose(psum_tr[:, 0:1],
                                               rl_tmp[0:1, cc * 128:(cc + 1) * 128],
                                               idf32[0:1, 0:1])
                        cnt["T"] += 1
                        t_l.append(("inc", sT.h, mtr, 1))
                        v_l.append(("wait", sT.h, base["T"] + cnt["T"]))
                        def mtc(e, qs=qs, cc=cc):
                            return e.tensor_copy(stats[:, 64 + 4 * qs + cc:
                                                       65 + 4 * qs + cc],
                                                 psum_tr[:, 0:1])
                        cnt["V"] += 1
                        v_l.append(("inc", sV.h, mtc, 1))
                        t_l.append(("wait", sV.h, base["V"] + cnt["V"]))
                    evac_after[qs] = cnt["V"]

            for i in range(len(sched)):
                emit_sblk(i)
                if i >= 1:
                    emit_oblk(i - 1)
            emit_oblk(len(sched) - 1)
            return t_l, v_l, s_l, cnt["T"], cnt["T2"], cnt["V"], cnt["S"]

        bodies = [attn_body(j) for j in range(4)]
        assert len({b[3:] for b in bodies}) == 1, [b[3:] for b in bodies]
        nT_a, nT2_a, nV_a, nS_a = bodies[0][3:]

        def play(e, items):
            for it in items:
                if it[0] == "wait":
                    e.wait_ge(it[1], it[2])
                elif it[0] == "do":
                    it[1](e)
                else:
                    it[2](e).then_inc(it[1], it[3])

        kv_ready_n = kv_ready
        qt_ready = proj_done

        def mk_attn(which):
            def run(eng):
                if which == 0:
                    eng.wait_ge(sGP.h, kv_ready_n)
                    eng.wait_ge(sV.h, qt_ready)
                pid = eng.partition_id()
                idx = eng.compute_val(pid % 4)
                for j in eng.Switch(idx, 4):
                    play(eng, bodies[j][which])
            return run

        P.on("tensor", mk_attn(0))
        P.on("vector", mk_attn(1))
        P.on("scalar", mk_attn(2))
        sT.n += nT_a
        sT2.n += nT2_a
        sV.n += nV_a
        sS.n += nS_a
        P.flush()

        P.wait("tensor", sV)
        ep_ev = {}
        ke = 0
        for s in range(4):
            for mq in range(4):
                ch = s * 4 + mq
                for ec in range(2):
                    b2 = ke % 2
                    if ke >= 2:
                        P.wait("tensor", sV2, ep_ev[ke - 2])
                    for dvc in range(2):
                        fn = lambda e, s=s, mq=mq, ec=ec, dvc=dvc, b2=b2: e.matmul(
                            atp[b2][:, :],
                            O_sb[:, s * 1024 + dvc * 512 + mq * 128:
                                 s * 1024 + dvc * 512 + mq * 128 + 128],
                            wvu_ga[:, dvc * E + ec * 512: dvc * E + (ec + 1) * 512],
                            start=(dvc == 0), stop=(dvc == 1))
                        if dvc == 1:
                            emit(P, "tensor", sT, fn)
                        else:
                            P.t(fn)
                    P.wait("vector", sT)
                    ep_ev[ke] = emit(P, "vector", sV2,
                         lambda e, ch=ch, ec=ec, b2=b2: e.scalar_tensor_tensor(
                             out=x_res[:, ch * E + ec * 512: ch * E + ec * 512 + 512],
                             in0=atp[b2][:, :], scalar=stats[:, 64 + ch:65 + ch],
                             in1=x_res[:, ch * E + ec * 512: ch * E + ec * 512 + 512],
                             op0=ALU.mult, op1=ALU.add))
                    ke += 1
        attn_done = sV2.n
        P.flush()
        st_p3.close()
        st_kv.close()
        st_att.close()

        st_ffn = ExitStack()
        wdT = st_ffn.enter_context(nc.sbuf_tensor("wdT", [128, NIC * 1024], BF16))
        sduv = st_ffn.enter_context(nc.sbuf_tensor("sduv", [128, 12], F32))
        st_p45 = ExitStack()
        scr5 = st_p45.enter_context(nc.sbuf_tensor("scr5", [128, E], F32))
        cb1_bc = st_p45.enter_context(nc.sbuf_tensor("cb1_bc", [128, E], F32))
        sb_bc = st_p45.enter_context(nc.sbuf_tensor("sb_bc", [128, E], F32))
        rowA5 = st_p45.enter_context(nc.sbuf_tensor("rowA5", [1, 1024], F32))
        rowB5 = st_p45.enter_context(nc.sbuf_tensor("rowB5", [1, 1024], F32))
        rowC5 = st_p45.enter_context(nc.sbuf_tensor("rowC5", [1, 1024], F32))
        ps_bc2 = pst("ps_bc2", [128, 512], st_p45)
        pp2 = [pst(f"pp2_{i}", [128, 512], st_p45) for i in range(2)]
        pw = [st_p45.enter_context(nc.psum_tensor(f"pw{i}", [128, 512], BF16))
              for i in range(2)]
        st_wd = ExitStack()
        wdn = [st_wd.enter_context(nc.sbuf_tensor(f"wdn{i}", [128, II], F32))
               for i in range(2)]
        wdq = st_wd.enter_context(nc.sbuf_tensor("wdq", [128, II], BF16))
        i8w = st_wd.enter_context(nc.sbuf_tensor("i8w", [128, II], I8))

        spill_of = {}
        for q in range(4):
            spill_of[q] = emit(P, "sync", sSP, lambda e, q=q: e.dma_start(
                out=bass.AP(x2_dram, q * 4 * 128 * E, [[E, 128], [128 * E, 4], [1, E]]),
                in_=x_res[:, q * 4 * E:(q + 1) * 4 * E]), 16)
            P.wait("sync", sSP, spill_of[q])

        wd_ld = {}
        for hc0 in range(2):
            wd_ld[hc0] = emit(P, "scalar", sX2[hc0 % 2], lambda e, hc0=hc0: e.dma_start(
                out=wdn[hc0 % 2][:, :],
                in_=bass.AP(wd_in.tensor, hc0 * 128 * II, [[II, 128], [1, II]])), 16)

        for it_i, (vi, dstt) in enumerate(((4, cb1_bc), (3, sb_bc), (5, None))):
            if it_i > 0:
                P.wait("gpsimd", sT)
                P.wait("gpsimd", sV)
            rn = emit(P, "gpsimd", sGP, lambda e, vi=vi: e.dma_start(
                out=rowA5[:, :], in_=bass.AP(mod_ag_o, vi * 1024, [[1, 1], [1, 1024]])), 16)
            P.wait("gpsimd", sGP, rn)
            P.wait("vector", sGP, rn)
            if vi == 4:
                emit(P, "vector", sV, lambda e: e.tensor_scalar(
                    out=rowA5[:, :], in0=rowA5[:, :], scalar1=1.0, scalar2=None,
                    op0=ALU.add))
            elif vi == 5:
                for fn in (lambda e: e.tensor_scalar(out=rowB5[:, :], in0=rowA5[:, :],
                                                     scalar1=0.0, scalar2=None,
                                                     op0=ALU.is_ge),
                           lambda e: e.tensor_scalar(out=rowB5[:, :], in0=rowB5[:, :],
                                                     scalar1=2.0, scalar2=-1.0,
                                                     op0=ALU.mult, op1=ALU.add),
                           lambda e: e.tensor_scalar(out=rowC5[:, :], in0=rowA5[:, :],
                                                     scalar1=-1.0, scalar2=None,
                                                     op0=ALU.mult),
                           lambda e: e.tensor_max(rowC5[:, :], rowC5[:, :], rowA5[:, :]),
                           lambda e: e.tensor_scalar(out=rowC5[:, :], in0=rowC5[:, :],
                                                     scalar1=0.25, scalar2=None,
                                                     op0=ALU.max)):
                    emit(P, "vector", sV, fn)
                    P.wait("vector", sV)
                emit(P, "vector", sV, lambda e: e.tensor_mul(gbrow[:, :], rowC5[:, :],
                                                             rowB5[:, :]))
                continue
            else:
                emit(P, "vector", sV, lambda e: e.tensor_copy(rowB5[:, :], rowA5[:, :]))
            P.wait("tensor", sV)
            for half in range(2):
                emit(P, "tensor", sT, lambda e, half=half: e.matmul(
                    ps_bc2[:, :], ones_f1[:, :], rowA5[0:1, half * 512:(half + 1) * 512],
                    start=True, stop=True))
                P.wait("vector", sT)
                emit(P, "vector", sV, lambda e, dstt=dstt, half=half: e.tensor_copy(
                    dstt[:, half * 512:(half + 1) * 512], ps_bc2[:, :]))
                P.wait("tensor", sV)

        for ch in range(NCH):
            fnr = lambda e, ch=ch: e.reduce_sum(
                out=sums[:, ch:ch + 1], in_=x_res[:, ch * E:(ch + 1) * E], axis=AX.X)
            if ch == NCH - 1:
                emit(P, "vector", sV, fnr)
            else:
                P.v(fnr)
            emit(P, "scalar", sS, lambda e, ch=ch: e.activation(
                out=scr5[:, :], in_=x_res[:, ch * E:(ch + 1) * E],
                func=AF.Square, accum_out=sums[:, 16 + ch:17 + ch]))
        sq2_done = sS.n
        P.wait("vector", sV)
        emit(P, "vector", sV, lambda e: e.tensor_scalar(
            out=stats[:, 32:48], in0=sums[:, 0:16],
            scalar1=1.0 / E, scalar2=None, op0=ALU.mult))
        P.wait("vector", sV)
        P.wait("vector", sS, sq2_done)
        emit(P, "vector", sV, lambda e: e.tensor_scalar(
            out=sums[:, 16:32], in0=sums[:, 16:32],
            scalar1=1.0 / E, scalar2=None, op0=ALU.mult))
        P.wait("vector", sV)
        emit(P, "vector", sV, lambda e: e.tensor_mul(stats[:, 80:96], stats[:, 32:48], stats[:, 32:48]))
        P.wait("vector", sV)
        emit(P, "vector", sV, lambda e: e.tensor_sub(stats[:, 48:64], sums[:, 16:32],
                                                     stats[:, 80:96]))
        P.wait("vector", sV)
        emit(P, "vector", sV, lambda e: e.tensor_scalar(
            out=stats[:, 48:64], in0=stats[:, 48:64], scalar1=EPS, scalar2=None, op0=ALU.add))
        P.wait("vector", sV)
        emit(P, "vector", sV, lambda e: e.reciprocal(out=stats[:, 80:96],
                                                     in_=stats[:, 48:64]))
        P.wait("vector", sV)
        P.wait("scalar", sV)
        r2_done = emit(P, "scalar", sS, lambda e: e.activation(
            out=stats[:, 48:64], in_=stats[:, 80:96], func=AF.Sqrt))

        gw_state = {"gw": 0, "wdq_free": 0}
        wd_ev = {}

        def wd_hc(hc):
            gw = gw_state["gw"]
            P.wait("vector", sX2[hc % 2], wd_ld[hc])
            emit(P, "vector", sV, lambda e, hc=hc: e.tensor_reduce(
                out=sduv[:, hc:hc + 1], in_=wdn[hc % 2][:, :], axis=AX.X, op=ALU.max,
                apply_absolute_value=True))
            P.wait("vector", sV)
            emit(P, "vector", sV, lambda e, hc=hc: e.reciprocal(
                out=sums[:, 1:2], in_=sduv[:, hc:hc + 1]))
            P.wait("vector", sV)
            wq_sc = emit(P, "vector", sV, lambda e: e.tensor_scalar(
                out=sums[:, 1:2], in0=sums[:, 1:2], scalar1=127.0, scalar2=None,
                op0=ALU.mult))
            P.wait("vector", sV)
            P.wait("scalar", sV, wq_sc)
            cast_n = emit(P, "scalar", sS, lambda e, hc=hc: e.activation(
                out=i8w[:, :], in_=wdn[hc % 2][:, :], func=AF.Copy, bias=0.0,
                scale=sums[:, 1:2]))
            if hc + 2 < 8:
                P.wait("scalar", sS, cast_n)
                wd_ld[hc + 2] = emit(P, "scalar", sX2[hc % 2], lambda e, hc=hc:
                                     e.dma_start(
                    out=wdn[hc % 2][:, :],
                    in_=bass.AP(wd_in.tensor, (hc + 2) * 128 * II,
                                [[II, 128], [1, II]])), 16)
            P.wait("vector", sS, cast_n)
            if hc >= 1:
                P.wait("vector", sT2, gw_state["wdq_free"])
            wq_done = emit(P, "vector", sV, lambda e: e.tensor_copy(wdq[:, :],
                                                                    i8w[:, :]))
            P.wait("tensor", sV, wq_done)
            for grp in range(6):
                ics = list(range(grp * 4, min(grp * 4 + 4, NIC)))
                pbw = gw % 2
                if gw >= 2:
                    P.wait("tensor", sS2, wd_ev[gw - 2])
                for j, ic in enumerate(ics):
                    fn = lambda e, ic=ic, j=j, pbw=pbw: e.transpose(
                        pw[pbw][:, j * 128:(j + 1) * 128],
                        wdq[:, ic * 128:(ic + 1) * 128], idb[:, :])
                    if j == len(ics) - 1:
                        emit(P, "tensor", sT2, fn)
                    else:
                        P.t(fn)
                P.wait("scalar", sT2)
                for j, ic in enumerate(ics):
                    wd_ev[gw] = emit(P, "scalar", sS2, lambda e, ic=ic, j=j, hc=hc,
                                     pbw=pbw: e.activation(
                        out=wdT[:, ic * 1024 + hc * 128: ic * 1024 + hc * 128 + 128],
                        in_=pw[pbw][:, j * 128:(j + 1) * 128],
                        func=AF.Copy, bias=0.0, scale=1.0))
                gw += 1
            gw_state["gw"] = gw
            gw_state["wdq_free"] = sT2.n

        P.wait("vector", sS, r2_done)
        p1s = {}
        for ch in range(NCH):
            if ch % 2 == 1 and ch // 2 < 8:
                wd_hc(ch // 2)
            if ch % 4 == 0:
                P.wait("vector", sSP, spill_of[ch // 4])
            P.v(lambda e, ch=ch: e.tensor_scalar(
                out=x_res[:, ch * E:(ch + 1) * E], in0=x_res[:, ch * E:(ch + 1) * E],
                scalar1=stats[:, 32 + ch:33 + ch], scalar2=stats[:, 48 + ch:49 + ch],
                op0=ALU.subtract, op1=ALU.mult))
            P.v(lambda e, ch=ch: e.tensor_mul(x_res[:, ch * E:(ch + 1) * E],
                                              x_res[:, ch * E:(ch + 1) * E],
                                              cb1_bc[:, :]))
            P.v(lambda e, ch=ch: e.tensor_add(x_res[:, ch * E:(ch + 1) * E],
                                              x_res[:, ch * E:(ch + 1) * E],
                                              sb_bc[:, :]))
            emit(P, "vector", sV, lambda e, ch=ch: e.tensor_reduce(
                out=sums[:, 0:1], in_=x_res[:, ch * E:(ch + 1) * E], axis=AX.X,
                op=ALU.max, apply_absolute_value=True))
            P.wait("vector", sV)
            emit(P, "vector", sV, lambda e: e.tensor_max(hmax[:, 0:1], hmax[:, 0:1],
                                                         sums[:, 0:1]))
            if ch % 4 == 3:
                p1s[ch // 4] = sV.n
        emit(P, "vector", sV, lambda e: e.tensor_scalar(
            out=sduv[:, 0:8], in0=sduv[:, 0:8], scalar1=QINV, scalar2=None,
            op0=ALU.mult))
        P.wait("vector", sV)
        P.wait("tensor", sV)
        emit(P, "tensor", sT, lambda e: e.transpose(psum_tr[0:1, :], hmax[:, 0:1],
                                                    idf32[:, :]))
        P.wait("vector", sT)
        emit(P, "vector", sV, lambda e: e.tensor_copy(rowbuf[:, :], psum_tr[0:1, :]))
        P.wait("vector", sV)
        emit(P, "vector", sV, lambda e: e.reduce_max(out=smax[0:1, 0:1],
                                                     in_=rowbuf[:, :], axis=AX.X))
        P.wait("vector", sV)
        P.wait("sync", sV)
        arxn = emit(P, "sync", sSY, lambda e: e.dma_start(out=arx_i.ap(),
                                                          in_=smax[:, :]), 16)
        P.wait("gpsimd", sSY, arxn)
        emit(P, "gpsimd", sCC, lambda e: e.collective_compute(
            "AllReduce", ALU.max, replica_groups=ALLC,
            ins=[arx_i.ap().opt()], outs=[arx_o.ap().opt()]))
        P.wait("gpsimd", sCC)
        mx_ready = emit(P, "gpsimd", sGP, lambda e: e.dma_start(
            out=sc11[0:1, 2:3], in_=bass.AP(arx_o, 0, [[1, 1], [1, 1]])), 16)
        P.wait("gpsimd", sGP, mx_ready)
        P.wait("scalar", sGP, mx_ready)
        emit(P, "scalar", sS, lambda e: e.activation(out=sc11[0:1, 3:4],
                                                     in_=sc11[0:1, 0:1], func=AF.Sigmoid))
        P.wait("vector", sS)
        emit(P, "vector", sV, lambda e: e.tensor_mul(sc11[0:1, 4:5], sc11[0:1, 3:4],
                                                     sc11[0:1, 2:3]))
        P.wait("vector", sV)
        emit(P, "vector", sV, lambda e: e.tensor_scalar(
            out=sc11[0:1, 4:5], in0=sc11[0:1, 4:5],
            scalar1=QINV, scalar2=None, op0=ALU.mult))
        P.wait("vector", sV)
        emit(P, "vector", sV, lambda e: e.reciprocal(out=sc11[0:1, 5:6],
                                                     in_=sc11[0:1, 4:5]))
        P.wait("vector", sV)
        P.wait("tensor", sV)
        for srcc, dstc in ((5, 0), (4, 2)):
            emit(P, "tensor", sT, lambda e, srcc=srcc: e.matmul(
                ps_bc2[:, 0:1], ones_f1[:, :], sc11[0:1, srcc:srcc + 1],
                start=True, stop=True))
            P.wait("vector", sT)
            emit(P, "vector", sV, lambda e, dstc=dstc: e.tensor_copy(
                scal[:, dstc:dstc + 1], ps_bc2[:, 0:1]))
            P.wait("vector", sV)
            P.wait("tensor", sV)
        scal_ready = sV.n

        ev2 = {}
        vc2 = {}
        g2 = 0
        for s in range(4):
            P.wait("tensor", sV, p1s[s])
            for fc in range(8):
                pb2 = g2 % 2
                if g2 >= 2:
                    P.wait("tensor", sS2, ev2[g2 - 2])
                for cc in range(4):
                    fn = lambda e, s=s, fc=fc, cc=cc, pb2=pb2: e.transpose(
                        pp2[pb2][:, cc * 128:(cc + 1) * 128],
                        x_res[:, (s * 4 + cc) * E + fc * 128:
                              (s * 4 + cc) * E + fc * 128 + 128],
                        idf32[:, :])
                    if cc == 3:
                        emit(P, "tensor", sT2, fn)
                    else:
                        P.t(fn)
                P.wait("scalar", sT2)
                if g2 == 0:
                    P.wait("scalar", sV, scal_ready)
                if g2 >= 2:
                    P.wait("scalar", sV2, vc2[g2 - 2])
                ev2[g2] = emit(P, "scalar", sS2, lambda e, pb2=pb2: e.activation(
                    out=i8big[:, pb2 * 512:pb2 * 512 + 512], in_=pp2[pb2][:, :],
                    func=AF.Copy, bias=0.0, scale=scal[:, 0:1]))
                P.wait("vector", sS2, ev2[g2])
                vc2[g2] = emit(P, "vector", sV2, lambda e, s=s, fc=fc, pb2=pb2:
                               e.tensor_copy(
                                   xT[:, fc * TOK + s * 512: fc * TOK + s * 512 + 512],
                                   i8big[:, pb2 * 512:pb2 * 512 + 512]))
                g2 += 1
        xq_done = sV2.n
        P.wait("tensor", sV2, xq_done)
        spill_n = sSP.n
        P.flush()
        st_wd.close()
        st_p45.close()
        st_x.close()

        st_p5 = ExitStack()
        gT = st_p5.enter_context(nc.sbuf_tensor("gT", [128, NIC * TOK], BF16))
        vfin = st_p5.enter_context(nc.sbuf_tensor("vfin", [128, E], F32))
        vtmp = st_p5.enter_context(nc.sbuf_tensor("vtmp", [128, 128], F32))
        ps_bc3 = pst("ps_bc3", [128, 512], st_p5)
        st_pa = ExitStack()
        wgn = [st_pa.enter_context(nc.sbuf_tensor(f"wgn{i}", [128, E], F32))
               for i in range(2)]
        wun = [st_pa.enter_context(nc.sbuf_tensor(f"wun{i}", [128, E], F32))
               for i in range(2)]
        wgq = st_pa.enter_context(nc.sbuf_tensor("wgq", [128, E], BF16))
        wuq = st_pa.enter_context(nc.sbuf_tensor("wuq", [128, E], BF16))
        i8w2 = st_pa.enter_context(nc.sbuf_tensor("i8w2", [128, E], I8))
        wgT = [st_pa.enter_context(nc.sbuf_tensor(f"wgT{i}", [128, E], BF16))
               for i in range(2)]
        wuT = [st_pa.enter_context(nc.sbuf_tensor(f"wuT{i}", [128, E], BF16))
               for i in range(2)]
        gtmp = st_pa.enter_context(nc.sbuf_tensor("gtmp", [128, 512], F32))
        gps = [pst(f"gps{i}", [128, 512], st_pa) for i in range(2)]
        ups = [pst(f"ups{i}", [128, 512], st_pa) for i in range(2)]
        pwa = [st_pa.enter_context(nc.psum_tensor(f"pwa{i}", [128, 512], BF16))
               for i in range(2)]

        P.wait("scalar", sSP, spill_n)
        P.wait("vector", sSP, spill_n)
        P.wait("gpsimd", sSP, spill_n)
        P.wait("tensor", sSP, spill_n)

        sT_after_ic = {}
        wT_ready = {}
        wA_ld = {}
        wA_ev = {}
        wA_state = {"ga": 0}
        wq_tr_done = {}
        wA_cp = {}

        def ffnA_load(ic):
            buf = ic % 2
            emit(P, "scalar", sY2[buf], lambda e, ic=ic, buf=buf: e.dma_start(
                out=wgn[buf][:, :],
                in_=bass.AP(wg_in.tensor, ic * 128 * E, [[E, 128], [1, E]])), 16)
            wA_ld[ic] = emit(P, "scalar", sY2[buf], lambda e, ic=ic, buf=buf:
                             e.dma_start(
                out=wun[buf][:, :],
                in_=bass.AP(wu_in.tensor, ic * 128 * E, [[E, 128], [1, E]])), 16)

        def ffnA_prep(ic):
            buf = ic % 2
            P.wait("vector", sY2[buf], wA_ld[ic])
            P.v(lambda e, buf=buf: e.tensor_reduce(
                out=sums[:, 2:3], in_=wgn[buf][:, :], axis=AX.X, op=ALU.max,
                apply_absolute_value=True))
            emit(P, "vector", sV, lambda e, buf=buf: e.tensor_reduce(
                out=sums[:, 3:4], in_=wun[buf][:, :], axis=AX.X, op=ALU.max,
                apply_absolute_value=True))
            P.wait("vector", sV)
            P.v(lambda e: e.reciprocal(out=sums[:, 4:5], in_=sums[:, 2:3]))
            emit(P, "vector", sV, lambda e: e.reciprocal(out=sums[:, 5:6],
                                                         in_=sums[:, 3:4]))
            P.wait("vector", sV)
            emit(P, "vector", sV, lambda e: e.tensor_scalar(
                out=sums[:, 4:6], in0=sums[:, 4:6],
                scalar1=127.0, scalar2=None, op0=ALU.mult))
            P.wait("vector", sV)
            P.v(lambda e, buf=buf: e.tensor_scalar(
                out=sduv[:, 8 + 2 * buf:9 + 2 * buf], in0=sums[:, 2:3],
                scalar1=scal[:, 2:3], scalar2=QINV, op0=ALU.mult, op1=ALU.mult))
            wsc = emit(P, "vector", sV, lambda e, buf=buf: e.tensor_scalar(
                out=sduv[:, 9 + 2 * buf:10 + 2 * buf], in0=sums[:, 3:4],
                scalar1=scal[:, 2:3], scalar2=QINV, op0=ALU.mult, op1=ALU.mult))
            P.wait("scalar", sV, wsc)
            emit(P, "scalar", sS, lambda e, buf=buf: e.activation(
                out=i8big[:, :], in_=wgn[buf][:, :], func=AF.Copy, bias=0.0,
                scale=sums[:, 4:5]))
            P.wait("vector", sS)
            if ic >= 1:
                P.wait("vector", sT2, wq_tr_done[ic - 1])
            emit(P, "vector", sV, lambda e: e.tensor_copy(wgq[:, :], i8big[:, :]))
            cast2 = emit(P, "scalar", sS, lambda e, buf=buf: e.activation(
                out=i8w2[:, :], in_=wun[buf][:, :], func=AF.Copy, bias=0.0,
                scale=sums[:, 5:6]))
            P.wait("vector", sS)
            emit(P, "vector", sV, lambda e: e.tensor_copy(wuq[:, :], i8w2[:, :]))
            wA_cp[ic] = sV.n
            return cast2

        def ffnA_transpose(ic):
            buf = ic % 2
            P.wait("tensor", sV, wA_cp[ic])
            if ic >= 2:
                P.wait("tensor", sT, sT_after_ic[ic - 2])
            for half, (wsrc, wdst) in enumerate(((wgq, wgT), (wuq, wuT))):
                for fg in range(2):
                    pba = wA_state["ga"] % 2
                    if wA_state["ga"] >= 2:
                        P.wait("tensor", sS2, wA_ev[wA_state["ga"] - 2])
                    for j in range(4):
                        fn = lambda e, wsrc=wsrc, fg=fg, j=j, pba=pba: e.transpose(
                            pwa[pba][:, j * 128:(j + 1) * 128],
                            wsrc[:, fg * 512 + j * 128: fg * 512 + j * 128 + 128],
                            idb[:, :])
                        if j == 3:
                            emit(P, "tensor", sT2, fn)
                        else:
                            P.t(fn)
                    P.wait("scalar", sT2)
                    wA_ev[wA_state["ga"]] = emit(
                        P, "scalar", sS2,
                        lambda e, wdst=wdst, buf=buf, fg=fg, pba=pba: e.activation(
                            out=wdst[buf][:, fg * 512:(fg + 1) * 512],
                            in_=pwa[pba][:, :], func=AF.Copy, bias=0.0, scale=1.0))
                    wA_state["ga"] += 1
            wT_ready[ic] = sS2.n
            wq_tr_done[ic] = sT2.n

        ffnA_load(0)
        ffnA_load(1)
        ffnA_prep(0)
        ffnA_transpose(0)
        ffnA_prep(1)
        ffnA_transpose(1)
        sttn = {}
        for ic in range(NIC):
            buf = ic % 2
            if ic + 2 < NIC:
                ffnA_load(ic + 2)
            P.wait("tensor", sS2, wT_ready[ic])
            for tt in range(4):
                pb = tt % 2
                gk = ic * 4 + tt
                if gk >= 2:
                    P.wait("tensor", sV2, sttn[gk - 2])
                for kc in range(8):
                    P.t(lambda e, buf=buf, kc=kc, tt=tt, pb=pb: e.matmul(
                        gps[pb][:, :], wgT[buf][:, kc * 128:(kc + 1) * 128],
                        xT[:, kc * TOK + tt * 512: kc * TOK + (tt + 1) * 512],
                        start=(kc == 0), stop=(kc == 7)))
                for kc in range(8):
                    fn = lambda e, buf=buf, kc=kc, tt=tt, pb=pb: e.matmul(
                        ups[pb][:, :], wuT[buf][:, kc * 128:(kc + 1) * 128],
                        xT[:, kc * TOK + tt * 512: kc * TOK + (tt + 1) * 512],
                        start=(kc == 0), stop=(kc == 7))
                    if kc == 7:
                        emit(P, "tensor", sT, fn)
                    else:
                        P.t(fn)
                P.wait("scalar", sT)
                if gk >= 1:
                    P.wait("scalar", sV2, sttn[gk - 1])
                emit(P, "scalar", sS, lambda e, pb=pb, buf=buf: e.activation(
                    out=gtmp[:, :], in_=gps[pb][:, :], func=AF.Silu,
                    scale=sduv[:, 8 + 2 * buf:9 + 2 * buf]))
                P.wait("vector", sS)
                sttn[gk] = emit(P, "vector", sV2,
                                lambda e, ic=ic, tt=tt, pb=pb, buf=buf:
                                e.scalar_tensor_tensor(
                    out=gT[:, ic * TOK + tt * 512: ic * TOK + (tt + 1) * 512],
                    in0=ups[pb][:, :],
                    scalar=sduv[:, 9 + 2 * buf:10 + 2 * buf],
                    in1=gtmp[:, :], op0=ALU.mult, op1=ALU.mult))
                if tt == 3:
                    emit(P, "vector", sV, lambda e, ic=ic: e.tensor_reduce(
                        out=sums[:, 6:7],
                        in_=gT[:, ic * TOK:(ic + 1) * TOK],
                        axis=AX.X, op=ALU.max, apply_absolute_value=True))
                    P.wait("vector", sV)
                    P.v(lambda e: e.tensor_max(hmax[:, 1:2], hmax[:, 1:2],
                                               sums[:, 6:7]))
            sT_after_ic[ic] = sT.n
            if ic >= 1 and ic + 1 < NIC:
                ffnA_transpose(ic + 1)
            if ic + 2 < NIC:
                ffnA_prep(ic + 2)
        P.wait("vector", sV)
        fin_max = emit(P, "vector", sV, lambda e: e.tensor_max(
            hmax[:, 1:2], hmax[:, 1:2], hmax[:, 1:2]))
        P.wait("vector", sV)
        ffnA_done = sV2.n
        P.flush()
        st_pa.close()

        st_p6 = ExitStack()
        scr6 = st_p6.enter_context(nc.sbuf_tensor("scr6", [128, 512], F32))
        x2sb = [st_p6.enter_context(nc.sbuf_tensor(f"x2sb{i}", [128, E], F32))
                for i in range(2)]
        ysb = [st_p6.enter_context(nc.sbuf_tensor(f"ysb{i}", [128, E], F32))
               for i in range(2)]
        dps = [pst(f"dps{i}", [128, 512], st_p6) for i in range(4)]
        ld_of = {}
        for ch0 in range(2):
            ld_of[ch0] = emit(P, "scalar", sX2[ch0], lambda e, ch0=ch0: e.dma_start(
                out=x2sb[ch0][:, :],
                in_=bass.AP(x2_dram, ch0 * 128 * E, [[E, 128], [1, E]])), 16)

        emit(P, "tensor", sT, lambda e: e.transpose(psum_tr[0:1, :], hmax[:, 1:2],
                                                    idf32[:, :]))
        P.wait("vector", sT)
        emit(P, "vector", sV, lambda e: e.tensor_copy(rowbuf[:, :], psum_tr[0:1, :]))
        P.wait("vector", sV)
        emit(P, "vector", sV, lambda e: e.reduce_max(out=smax[0:1, 0:1],
                                                     in_=rowbuf[:, :], axis=AX.X))
        P.wait("vector", sV)
        P.wait("sync", sV)
        argn = emit(P, "sync", sSY, lambda e: e.dma_start(out=arg_i.ap(),
                                                          in_=smax[:, :]), 16)
        P.wait("gpsimd", sSY, argn)
        emit(P, "gpsimd", sCC, lambda e: e.collective_compute(
            "AllReduce", ALU.max, replica_groups=ALLC,
            ins=[arg_i.ap().opt()], outs=[arg_o.ap().opt()]))
        P.wait("gpsimd", sCC)
        mg_ready = emit(P, "gpsimd", sGP, lambda e: e.dma_start(
            out=sc11[0:1, 6:7], in_=bass.AP(arg_o, 0, [[1, 1], [1, 1]])), 16)
        P.wait("gpsimd", sGP, mg_ready)
        P.wait("scalar", sGP, mg_ready)
        emit(P, "scalar", sS, lambda e: e.activation(out=sc11[0:1, 7:8],
                                                     in_=sc11[0:1, 1:2], func=AF.Sigmoid))
        P.wait("vector", sS)
        emit(P, "vector", sV, lambda e: e.tensor_mul(sc11[0:1, 8:9], sc11[0:1, 7:8],
                                                     sc11[0:1, 6:7]))
        P.wait("vector", sV)
        emit(P, "vector", sV, lambda e: e.tensor_scalar(
            out=sc11[0:1, 8:9], in0=sc11[0:1, 8:9],
            scalar1=QINV, scalar2=None, op0=ALU.mult))
        P.wait("vector", sV)
        emit(P, "vector", sV, lambda e: e.reciprocal(out=sc11[0:1, 9:10],
                                                     in_=sc11[0:1, 8:9]))
        P.wait("vector", sV)
        P.wait("tensor", sV)
        for srcc, dstc in ((9, 1), (8, 3)):
            emit(P, "tensor", sT, lambda e, srcc=srcc: e.matmul(
                ps_bc3[:, 0:1], ones_f1[:, :], sc11[0:1, srcc:srcc + 1],
                start=True, stop=True))
            P.wait("vector", sT)
            emit(P, "vector", sV, lambda e, dstc=dstc: e.tensor_copy(
                scal[:, dstc:dstc + 1], ps_bc3[:, 0:1]))
            P.wait("vector", sV)
            P.wait("tensor", sV)

        P.wait("tensor", sV)
        for half in range(2):
            emit(P, "tensor", sT, lambda e, half=half: e.matmul(
                ps_bc3[:, 0:512], ones_f1[:, :], gbrow[0:1, half * 512:(half + 1) * 512],
                start=True, stop=True))
            P.wait("vector", sT)
            emit(P, "vector", sV, lambda e, half=half: e.tensor_copy(
                vfin[:, half * 512:(half + 1) * 512], ps_bc3[:, 0:512]))
            P.wait("tensor", sV)
        for hc in range(8):
            bcast_piece(sduv[:, hc:hc + 1], vtmp[:, :], ps_bc3[:, 0:128])
            P.v(lambda e, hc=hc: e.tensor_mul(
                vfin[:, hc * 128:(hc + 1) * 128], vfin[:, hc * 128:(hc + 1) * 128],
                vtmp[:, :]))
        vfin_done = emit(P, "vector", sV, lambda e: e.tensor_scalar(
            out=vfin[:, :], in0=vfin[:, :], scalar1=scal[:, 3:4], scalar2=None,
            op0=ALU.mult))

        qs_ev = {}
        qb_ev = {}
        qb_after = {}
        qk_state = {"k": 0}

        def quant_tt(tt):
            for ic in range(NIC):
                kq = qk_state["k"]
                h = kq % 2
                if kq >= 2:
                    P.wait("scalar", qb_ev[kq - 2][0], qb_ev[kq - 2][1])
                qs_ev[kq] = emit(P, "scalar", sS, lambda e, ic=ic, tt=tt, h=h:
                                 e.activation(
                    out=i8big[:, h * 512:(h + 1) * 512],
                    in_=gT[:, ic * TOK + tt * 512: ic * TOK + (tt + 1) * 512],
                    func=AF.Copy, bias=0.0, scale=scal[:, 1:2]))
                P.wait("vector", sS, qs_ev[kq])
                bn = emit(P, "vector", sV, lambda e, ic=ic, tt=tt, h=h: e.tensor_copy(
                    gT[:, ic * TOK + tt * 512: ic * TOK + (tt + 1) * 512],
                    i8big[:, h * 512:(h + 1) * 512]))
                qb_ev[kq] = (sV, bn)
                qk_state["k"] += 1
            qb_after[tt] = sV.n

        P.wait("scalar", sV)
        store_of = {}
        ep_of = {}
        quant_tt(0)
        for ch in range(NCH):
            tt = ch // 4
            xb2 = ch % 2
            if ch % 4 == 1 and tt + 1 < 4:
                quant_tt(tt + 1)
            if ch >= 2:
                P.wait("scalar", sV2, ep_of[ch - 2])
                ld_of[ch] = emit(P, "scalar", sX2[xb2], lambda e, ch=ch, xb2=xb2:
                                 e.dma_start(
                    out=x2sb[xb2][:, :],
                    in_=bass.AP(x2_dram, ch * 128 * E, [[E, 128], [1, E]])), 16)
            if ch % 4 == 0:
                P.wait("tensor", sV, qb_after[tt])
            if ch >= 2:
                P.wait("tensor", sV2, ep_of[ch - 2])
            for ic in range(NIC):
                for ec in range(2):
                    fn = lambda e, ch=ch, ec=ec, ic=ic, xb2=xb2: e.matmul(
                        dps[2 * xb2 + ec][:, :],
                        gT[:, ic * TOK + ch * 128: ic * TOK + (ch + 1) * 128],
                        wdT[:, ic * 1024 + ec * 512: ic * 1024 + (ec + 1) * 512],
                        start=(ic == 0), stop=(ic == NIC - 1))
                    if ic == NIC - 1 and ec == 1:
                        emit(P, "tensor", sT, fn)
                    else:
                        P.t(fn)
            P.wait("vector", sT)
            P.wait("vector", sX2[xb2], ld_of[ch])
            if ch >= 2:
                P.wait("vector", sY2[xb2], store_of[ch - 2])
            for ec in range(2):
                P.v(lambda e, ec=ec, xb2=xb2: e.tensor_mul(
                    scr6[:, :], dps[2 * xb2 + ec][:, :],
                    vfin[:, ec * 512:(ec + 1) * 512]))
                ep_of[ch] = emit(P, "vector", sV2,
                                 lambda e, ec=ec, xb2=xb2: e.tensor_add(
                                     ysb[xb2][:, ec * 512:(ec + 1) * 512],
                                     scr6[:, :],
                                     x2sb[xb2][:, ec * 512:(ec + 1) * 512]))
            P.wait("sync", sV2)
            store_of[ch] = emit(P, "sync", sY2[xb2], lambda e, ch=ch, xb2=xb2:
                                e.dma_start(
                out=bass.AP(y_out.tensor, ch * 128 * E, [[E, 128], [1, E]]),
                in_=ysb[xb2][:, :]), 16)
        P.wait("sync", sY2[0], sY2[0].n)
        P.wait("sync", sY2[1], sY2[1].n)
        P.flush()
        st_p6.close()
        st_p5.close()
        st_ffn.close()
    return nc


_NC_CACHE = {}


def kernel(L0_attn_inputs, L0_attn_Wq, L0_attn_Wk, L0_attn_Wvdown, L0_attn_Wvup,
           L0_ffn_clip_x, L0_ffn_clip_down, L0_ffn_w_gate, L0_ffn_w_up,
           L0_ffn_w_down, L0_c, L0_wmod):
    x = np.ascontiguousarray(np.asarray(L0_attn_inputs, dtype=np.float32))
    wq = np.ascontiguousarray(np.asarray(L0_attn_Wq, dtype=np.float32))
    wk = np.ascontiguousarray(np.asarray(L0_attn_Wk, dtype=np.float32))
    wvd = np.ascontiguousarray(np.asarray(L0_attn_Wvdown, dtype=np.float32))
    wvu = np.ascontiguousarray(np.asarray(L0_attn_Wvup, dtype=np.float32))
    cx = np.asarray(L0_ffn_clip_x, dtype=np.float32).reshape(1, 1)
    cd = np.asarray(L0_ffn_clip_down, dtype=np.float32).reshape(1, 1)
    wg = np.ascontiguousarray(np.asarray(L0_ffn_w_gate, dtype=np.float32))
    wu = np.ascontiguousarray(np.asarray(L0_ffn_w_up, dtype=np.float32))
    wd = np.ascontiguousarray(np.asarray(L0_ffn_w_down, dtype=np.float32))
    cc = np.ascontiguousarray(np.asarray(L0_c, dtype=np.float32))
    wmod = np.ascontiguousarray(np.asarray(L0_wmod, dtype=np.float32))

    if "nc" not in _NC_CACHE:
        _NC_CACHE["nc"] = build()
    nc = _NC_CACHE["nc"]

    in_maps = []
    for c in range(8):
        b, j = c // 4, c % 4
        x_own = np.ascontiguousarray(
            np.concatenate([x[b, 512 * g:512 * (g + 1), :] for g in GSTS[j]], axis=0))
        in_maps.append({
            "x_own": x_own,
            "wq": wq, "wk": wk, "wvd": wvd, "wvu": wvu,
            "clip_x": cx, "clip_down": cd,
            "w_gate": wg, "w_up": wu, "w_down": wd,
            "c_b": np.ascontiguousarray(cc[b]),
            "wmod_sl": np.ascontiguousarray(wmod[:, 1536 * j:1536 * (j + 1)]),
        })
    res = run_bass_kernel_spmd(nc, in_maps, core_ids=list(range(8)))
    y = np.empty((B, N, E), dtype=np.float32)
    for c in range(8):
        b, j = c // 4, c % 4
        yo = res.results[c]["y_own"]
        for s, g in enumerate(GSTS[j]):
            y[b, 512 * g:512 * (g + 1), :] = yo[512 * s:512 * (s + 1), :]
    return y
